# version 02
# baseline (speedup 1.0000x reference)
import math
from collections import defaultdict

import numpy as np

import concourse.bass as bass
import concourse.mybir as mybir
from concourse.bass_utils import run_bass_kernel_spmd

F32 = mybir.dt.float32
F16 = mybir.dt.float16
BF16 = mybir.dt.bfloat16
AF = mybir.ActivationFunctionType
ALU = mybir.AluOpType
AX = mybir.AxisListType

B = 2
S = 8192
D = 2048
TOK = 2048
NT = TOK // 128
KB = 1024
NEG = -1.0e30
LAM_INIT = 0.8 - 0.6 * math.exp(-0.3 * 1)
DSC = 128.0 ** -0.5
LNEPS = 1e-5



def _sylvester(n):
    h = np.array([[1.0]], np.float32)
    while h.shape[0] < n:
        h = np.block([[h, h], [h, -h]]).astype(np.float32)
    return h


def _slopes16():
    start = 2 ** (-2 ** (-(math.log2(16) - 3)))
    return np.array([start * start ** i for i in range(16)], np.float64)


def _diag_masks(g):
    m = np.zeros((2, 128, KB), np.float32)
    for par in range(2):
        off = 128 * g + 512 * par
        for qi in range(128):
            m[par, qi, off + qi + 1:] = NEG
    return m


def _alibi_rel():
    sl = _slopes16()
    k = np.arange(S, dtype=np.float64)
    kmax = (k // KB) * KB + (KB - 1)
    a = -sl[:, None] * (kmax[None] - k[None])
    return np.ascontiguousarray(a.astype(np.float16))


def _alibi_shift(h, kb):
    sl = _slopes16()
    return float(-sl[h] * (S - 1 - (kb * KB + KB - 1)))



ENGS = ("sync", "gpsimd", "tensor", "vector", "scalar")


class Prog:
    def __init__(self, nc):
        self.nc = nc
        self.qs = {e: [] for e in ENGS}
        self.sems = {}
        self.cnt = defaultdict(int)
        self._guards = []

    def sem(self, name):
        if name not in self.sems:
            g = self.nc.semaphore(name)
            self.sems[name] = g.__enter__()
            self._guards.append(g)
        return self.sems[name]

    def wait(self, eng, sem, val=None):
        if val is None:
            val = self.cnt[sem]
        if val <= 0:
            return
        s = self.sem(sem)
        self.qs[eng].append(lambda E, s=s, v=val: E.wait_ge(s, v))

    def step(self, eng, build_fn, inc=None, inc_by=1):
        if inc is not None:
            s = self.sem(inc)
            self.cnt[inc] += inc_by

            def gg(E, f=build_fn, s=s, k=inc_by):
                f(E).then_inc(s, k)
            self.qs[eng].append(gg)
            return self.cnt[inc]
        self.qs[eng].append(build_fn)
        return None

    def barrier(self, engines=ENGS):
        snap = dict(self.cnt)
        for e in engines:
            for s, v in snap.items():
                if v > 0:
                    self.wait(e, s, v)

    def finalize(self):
        with self.nc.Block() as block:
            @block.sync
            def _(E):
                for f in self.qs["sync"]:
                    f(E)

            @block.gpsimd
            def _(E):
                for f in self.qs["gpsimd"]:
                    f(E)

            @block.tensor
            def _(E):
                for f in self.qs["tensor"]:
                    f(E)

            @block.vector
            def _(E):
                for f in self.qs["vector"]:
                    f(E)

            @block.scalar
            def _(E):
                for f in self.qs["scalar"]:
                    f(E)
        return self.nc



def emit_row_broadcast(p, ps_f32, ones_b, row_ap, out_sb, width):
    for i in range(0, width, 512):
        w = min(512, width - i)
        p.wait("tensor", "ve")
        p.step("tensor", lambda E, i=i, w=w: E.matmul(
            ps_f32[:, 0:w], ones_b[0:1, 0:128], row_ap[0:1, i:i + w],
            start=True, stop=True), inc="pe")
        p.wait("vector", "pe")
        p.step("vector", lambda E, i=i, w=w: E.tensor_copy(
            out_sb[:, i:i + w], ps_f32[:, 0:w]), inc="ve")
    p.wait("tensor", "ve")


def emit_tm2fm(p, src_bf, x_fm, ident, nsub, ps_tr, tcols=None, src_stride=D):
    if tcols is None:
        tcols = nsub * 128
    for c in range(16):
        for st0 in range(0, nsub, 4):
            n = min(4, nsub - st0)
            p.wait("tensor", "ve")
            for u in range(n):
                p.step("tensor", lambda E, c=c, st=st0, u=u: E.transpose(
                    ps_tr[:, u * 128:(u + 1) * 128],
                    src_bf[:, (st + u) * src_stride + c * 128:
                           (st + u) * src_stride + c * 128 + 128],
                    ident[:, :]), inc="pe" if u == n - 1 else None)
            p.wait("vector", "pe")
            p.step("vector", lambda E, c=c, st=st0, n=n: E.tensor_copy(
                x_fm[:, c * tcols + st * 128: c * tcols + (st + n) * 128],
                ps_tr[:, 0:n * 128]), inc="ve")
    p.wait("tensor", "ve")


def emit_ln_epilogue(p, ysb, sq, xo_sb, mst, width=D):
    p.wait("vector", "ve")
    p.step("vector", lambda E: E.reduce_sum(
        mst[:, 0:1], ysb[:, 0:width], axis=AX.X), inc="ve")
    p.wait("vector", "ve")
    p.step("vector", lambda E: E.tensor_scalar_mul(
        mst[:, 1:2], mst[:, 0:1], -1.0 / width), inc="ve")
    p.wait("scalar", "ve")
    p.step("scalar", lambda E: E.activation(
        sq[:, 0:width], ysb[:, 0:width], AF.Square,
        bias=mst[:, 1:2], accum_out=mst[:, 2:3]), inc="se")
    p.wait("vector", "se")
    p.step("vector", lambda E: E.tensor_scalar(
        mst[:, 2:3], mst[:, 2:3], 1.0 / width, LNEPS,
        ALU.mult, ALU.add), inc="ve")
    p.wait("scalar", "ve")
    p.step("scalar", lambda E: E.activation(
        mst[:, 3:4], mst[:, 2:3], AF.Sqrt), inc="se")
    p.wait("vector", "se")
    p.step("vector", lambda E: E.reciprocal(mst[:, 4:5], mst[:, 3:4]), inc="ve")
    p.wait("vector", "ve")
    p.step("vector", lambda E: E.tensor_mul(
        mst[:, 1:2], mst[:, 1:2], mst[:, 4:5]), inc="ve")
    p.wait("scalar", "ve")
    p.step("scalar", lambda E: E.activation(
        xo_sb[:, 0:width], ysb[:, 0:width], AF.Identity,
        scale=mst[:, 4:5], bias=mst[:, 1:2]), inc="se")



def build_attention_nc(layer, dev=False):
    nc = bass.Bass("TRN2", target_bir_lowering=False, debug=False)
    p = Prog(nc)

    xb = nc.dram_tensor("xb", [S, D], F32, kind="ExternalInput").ap()
    xq = nc.dram_tensor("xq", [TOK, D], F32, kind="ExternalInput").ap()
    ident_in = nc.dram_tensor("ident", [128, 128], F32, kind="ExternalInput").ap()
    masks_in = nc.dram_tensor("masks", [2, 128, KB], F32, kind="ExternalInput").ap()
    b_qkv = nc.dram_tensor("bqkv", [128, 32], F32, kind="ExternalInput").ap()
    bv_row = nc.dram_tensor("bv_row", [1, D], F32, kind="ExternalInput").ap()
    b_out = nc.dram_tensor("bout", [1, D], F32, kind="ExternalInput").ap()
    xout = nc.dram_tensor("xout", [TOK, D], F32, kind="ExternalOutput").ap()

    if layer == 0:
        w_all = nc.dram_tensor("wqkv_t", [D, 4096], F32, kind="ExternalInput").ap()
        w_q = w_k = w_v = w_all
        w_o = nc.dram_tensor("wout_t", [D, D], F32, kind="ExternalInput").ap()
        lam_in = nc.dram_tensor("lamv", [1, 512], F32, kind="ExternalInput").ap()
        subw_in = nc.dram_tensor("subw", [1, 256], F32, kind="ExternalInput").ap()
        alibi_in = None
        KF, VF, EDIM, NUNIT = 1024, 1024, 256, 8
        qcol0, kcol0, vcol0 = 0, 2048, 3072
    else:
        w_q = nc.dram_tensor("wq", [D, D], F32, kind="ExternalInput").ap()
        w_k = nc.dram_tensor("wk", [D, D], F32, kind="ExternalInput").ap()
        w_v = nc.dram_tensor("wv", [D, D], F32, kind="ExternalInput").ap()
        w_o = nc.dram_tensor("wo", [D, D], F32, kind="ExternalInput").ap()
        alibi_in = nc.dram_tensor("alibi_f", [16, S], F16,
                                  kind="ExternalInput").ap()
        KF, VF, EDIM, NUNIT = 2048, 2048, 128, 16
        qcol0, kcol0, vcol0 = 0, 0, 0

    k_dram = nc.dram_tensor("k_scr", [KF, S], BF16, kind="Internal")
    v_dram = nc.dram_tensor("v_scr", [S, VF], BF16, kind="Internal")
    o_dram = nc.dram_tensor("o_scr", [TOK, D], BF16, kind="Internal")

    TB = 512
    NSUB = TB // 128

    ident = nc.alloc_sbuf_tensor("identb", [128, 128], BF16)
    biasf = nc.alloc_sbuf_tensor("biasf", [128, 32], F32)
    qb_sc = nc.alloc_sbuf_tensor("qb_sc", [128, 16], F32)
    ones = nc.alloc_sbuf_tensor("ones", [1, 128], F32)
    row = nc.alloc_sbuf_tensor("row", [1, D], F32)
    bob = nc.alloc_sbuf_tensor("bob", [128, D], F32)
    bvb = nc.alloc_sbuf_tensor("bvb", [128, VF], F32)
    mst = nc.alloc_sbuf_tensor("mst", [128, 16], F32)
    lamn = nc.alloc_sbuf_tensor("lamn", [128, 1], F32)
    subwb = nc.alloc_sbuf_tensor("subwb", [128, 256], F32)
    obufb = nc.alloc_sbuf_tensor("obufb", [128, 256], BF16)
    obuf = nc.alloc_sbuf_tensor("obuf", [128, 256], F32)
    qall = nc.alloc_sbuf_tensor("qall", [128, 16 * TOK], BF16)

    ps_s0 = nc.alloc_psum_tensor("ps_s0", [128, KB], F32)
    ps_s1 = nc.alloc_psum_tensor("ps_s1", [128, KB], F32)
    ps_et0 = nc.alloc_psum_tensor("ps_et0", [128, KB], BF16)
    ps_et1 = nc.alloc_psum_tensor("ps_et1", [128, KB], BF16)
    ps_o = nc.alloc_psum_tensor("ps_o", [128, 1024], F32)
    psS = [ps_s0, ps_s1]
    psE = [ps_et0, ps_et1]
    ps_tr = ps_et0

    p.step("gpsimd", lambda E: E.dma_start(out=ident[:, :], in_=ident_in[:, :]),
           inc="gd", inc_by=16)
    p.step("sync", lambda E: E.dma_start(out=biasf[:, :], in_=b_qkv[:, :]),
           inc="ld", inc_by=16)
    p.step("sync", lambda E: E.dma_start(out=row[:, :], in_=b_out[:, :]),
           inc="ld", inc_by=16)
    p.step("vector", lambda E: E.memset(ones[:, :], 1.0), inc="ve")
    p.wait("vector", "ld")
    p.wait("tensor", "ld")
    p.wait("tensor", "gd")
    p.step("vector", lambda E: E.tensor_scalar_mul(
        qb_sc[:, :], biasf[:, 0:16], DSC), inc="ve")
    p.wait("tensor", "ve")
    emit_row_broadcast(p, ps_o, ones, row, bob, D)
    p.wait("sync", "pe")
    p.step("sync", lambda E: E.dma_start(out=row[0:1, 0:VF],
                                         in_=bv_row[0:1, 0:VF]),
           inc="ld", inc_by=16)
    p.wait("tensor", "ld")
    emit_row_broadcast(p, ps_o, ones, row, bvb, VF)

    if layer == 0:
        lam_sb = nc.alloc_sbuf_tensor("lam_sb", [1, 512], F32)
        subr = nc.alloc_sbuf_tensor("subr", [1, 256], F32)
        sm = nc.alloc_sbuf_tensor("sm", [1, 8], F32)

        p.step("sync", lambda E: E.dma_start(out=lam_sb[:, :], in_=lam_in[:, :]),
               inc="ld", inc_by=16)
        p.step("sync", lambda E: E.dma_start(out=subr[:, :], in_=subw_in[:, :]),
               inc="ld", inc_by=16)
        p.wait("vector", "ld")
        p.step("vector", lambda E: E.tensor_mul(
            lam_sb[0:1, 0:128], lam_sb[0:1, 0:128], lam_sb[0:1, 128:256]),
            inc="ve")
        p.step("vector", lambda E: E.tensor_mul(
            lam_sb[0:1, 256:384], lam_sb[0:1, 256:384], lam_sb[0:1, 384:512]),
            inc="ve")
        p.wait("vector", "ve")
        p.step("vector", lambda E: E.reduce_sum(
            sm[0:1, 0:1], lam_sb[0:1, 0:128], axis=AX.X), inc="ve")
        p.step("vector", lambda E: E.reduce_sum(
            sm[0:1, 1:2], lam_sb[0:1, 256:384], axis=AX.X), inc="ve")
        p.wait("scalar", "ve")
        p.step("scalar", lambda E: E.activation(
            sm[0:1, 2:3], sm[0:1, 0:1], AF.Exp), inc="se")
        p.step("scalar", lambda E: E.activation(
            sm[0:1, 3:4], sm[0:1, 1:2], AF.Exp), inc="se")
        p.wait("vector", "se")
        p.step("vector", lambda E: E.tensor_sub(
            sm[0:1, 4:5], sm[0:1, 3:4], sm[0:1, 2:3]), inc="ve")
        p.wait("vector", "ve")
        p.step("vector", lambda E: E.tensor_scalar_add(
            sm[0:1, 4:5], sm[0:1, 4:5], -LAM_INIT), inc="ve")
        p.wait("tensor", "ve")
        p.step("tensor", lambda E: E.matmul(
            ps_o[:, 0:1], ones[0:1, 0:128], sm[0:1, 4:5],
            start=True, stop=True), inc="pe")
        p.wait("vector", "pe")
        p.step("vector", lambda E: E.tensor_copy(lamn[:, :], ps_o[:, 0:1]),
               inc="ve")
        p.step("vector", lambda E: E.tensor_scalar_mul(
            subr[:, :], subr[:, :], 1.0 - LAM_INIT), inc="ve")
        p.wait("tensor", "ve")
        emit_row_broadcast(p, ps_o, ones, subr, subwb, 256)

    n_kchunk = KF // 128
    pe_use_marks = {}

    with (
        nc.sbuf_tensor("x_tm", [128, NSUB * D], F32) as x_tm,
        nc.sbuf_tensor("x_tmb", [128, NSUB * D], BF16) as x_tmb,
        nc.sbuf_tensor("x_fm", [128, 16 * TB], BF16) as x_fm,
        nc.sbuf_tensor("wbuf", [128, 2 * 16 * 128], BF16) as wbuf,
        nc.sbuf_tensor("wvbuf", [128, 2 * 16 * 512], BF16) as wvbuf,
        nc.sbuf_tensor("evac", [128, 2 * 512], BF16) as evac,
    ):
        ev_ctr = [0]
        ev_marks = {}

        def ev_half():
            i = ev_ctr[0]
            ev_ctr[0] += 1
            return i, (i % 2) * 512

        def load_w_lhsT(half, w_ap, fbase):
            p.step("gpsimd", lambda E, half=half, w_ap=w_ap, fbase=fbase:
                   E.dma_start(
                       out=wbuf[:, half * 2048:(half + 1) * 2048],
                       in_=bass.AP(w_ap.tensor, fbase,
                                   [[w_ap.shape[1], 128],
                                    [128 * w_ap.shape[1], 16],
                                    [1, 128]])), inc="gd", inc_by=16)

        def load_w_rhs(half, w_ap, fbase, fw=512):
            p.step("gpsimd", lambda E, half=half, w_ap=w_ap, fbase=fbase, fw=fw:
                   E.dma_start(
                       out=wvbuf[:, half * 8192: half * 8192 + 16 * fw],
                       in_=bass.AP(w_ap.tensor, fbase,
                                   [[w_ap.shape[1], 128],
                                    [128 * w_ap.shape[1], 16],
                                    [1, fw]])), inc="gd", inc_by=16)

        def fm_matmul_into(ps, wbh, last_inc=True):
            for c in range(16):
                p.step("tensor", lambda E, c=c, ps=ps, wbh=wbh: E.matmul(
                    ps[:, 0:TB],
                    wbuf[:, wbh * 2048 + c * 128: wbh * 2048 + (c + 1) * 128],
                    x_fm[:, c * TB:(c + 1) * TB],
                    start=(c == 0), stop=(c == 15)),
                    inc="pe" if (c == 15 and last_inc) else None)

        for tb in range(S // TB + 4):
            is_q = tb >= S // TB
            qb = tb - S // TB
            p.wait("sync", "ve")
            p.wait("sync", "pe")
            if not is_q:
                p.step("sync", lambda E, tb=tb: E.dma_start(
                    out=x_tm[:, :],
                    in_=bass.AP(xb.tensor, tb * TB * D,
                                [[D, 128], [128 * D, NSUB], [1, D]])),
                    inc="ld", inc_by=16)
            else:
                p.step("sync", lambda E, qb=qb: E.dma_start(
                    out=x_tm[:, :],
                    in_=bass.AP(xq.tensor, qb * TB * D,
                                [[D, 128], [128 * D, NSUB], [1, D]])),
                    inc="ld", inc_by=16)
            p.wait("vector", "ld")
            p.wait("vector", "pe")
            for st in range(NSUB):
                p.step("vector", lambda E, st=st: E.tensor_copy(
                    x_tmb[:, st * D:(st + 1) * D], x_tm[:, st * D:(st + 1) * D]),
                    inc="ve")
            p.wait("tensor", "ve")
            emit_tm2fm(p, x_tmb, x_fm, ident, NSUB, ps_tr[:, 0:512], tcols=TB)
            xfm_ok = p.cnt["ve"]

            if not is_q:
                for kc in range(n_kchunk):
                    half = kc % 2
                    p.wait("gpsimd", "pe")
                    load_w_lhsT(half, w_k, kcol0 + kc * 128)
                    p.wait("tensor", "gd")
                    p.wait("tensor", "ve", xfm_ok)
                    p.wait("tensor", "se")
                    fm_matmul_into(ps_s0, half)
                    pe_use_marks[("k", half)] = p.cnt["pe"]
                    p.wait("scalar", "pe")
                    p.step("scalar", lambda E, kc=kc: E.activation(
                        evac[:, 0:TB], ps_s0[:, 0:TB], AF.Identity,
                        bias=biasf[:, 16 + kc:17 + kc]), inc="se")
                    p.wait("sync", "se")
                    p.step("sync", lambda E, kc=kc, tb=tb: E.dma_start(
                        out=bass.AP(k_dram, kc * 128 * S + tb * TB,
                                    [[S, 128], [1, TB]]),
                        in_=evac[:, 0:TB]), inc="st", inc_by=16)
                for vg in range(VF // 512):
                    half = vg % 2
                    p.wait("gpsimd", "pe")
                    load_w_rhs(half, w_v, vcol0 + vg * 512)
                    p.wait("tensor", "gd")
                    p.wait("tensor", "ve", xfm_ok)
                    for st in range(NSUB):
                        p.wait("tensor", "ve")
                        for c in range(16):
                            p.step("tensor",
                                   lambda E, c=c, st=st, half=half: E.matmul(
                                       ps_s1[:, 0:512],
                                       x_fm[:, c * TB + st * 128:
                                            c * TB + (st + 1) * 128],
                                       wvbuf[:, half * 8192 + c * 512:
                                             half * 8192 + (c + 1) * 512],
                                       start=(c == 0), stop=(c == 15)),
                                   inc="pe" if c == 15 else None)
                        if st == NSUB - 1:
                            pe_use_marks[("v", half)] = p.cnt["pe"]
                        p.wait("vector", "pe")
                        ei, eo = ev_half()
                        p.wait("vector", "st", ev_marks.get(ei - 2, 0))
                        p.step("vector", lambda E, vg=vg, eo=eo: E.tensor_add(
                            evac[:, eo:eo + 512], ps_s1[:, 0:512],
                            bvb[:, vg * 512:(vg + 1) * 512]), inc="ve")
                        p.wait("sync", "ve")
                        ev_marks[ei] = p.step("sync",
                            lambda E, vg=vg, tb=tb, st=st, eo=eo: E.dma_start(
                                out=bass.AP(v_dram,
                                            (tb * TB + st * 128) * VF + vg * 512,
                                            [[VF, 128], [1, 512]]),
                                in_=evac[:, eo:eo + 512]), inc="st", inc_by=16)
            else:
                for qc in range(16):
                    half = qc % 2
                    p.wait("gpsimd", "pe")
                    load_w_lhsT(half, w_q, qcol0 + qc * 128)
                    p.wait("tensor", "gd")
                    p.wait("tensor", "ve", xfm_ok)
                    p.wait("tensor", "se")
                    fm_matmul_into(ps_s0, half)
                    pe_use_marks[("k", half)] = p.cnt["pe"]
                    p.wait("scalar", "pe")
                    p.step("scalar", lambda E, qc=qc, qb=qb: E.activation(
                        qall[:, qc * TOK + qb * TB: qc * TOK + (qb + 1) * TB],
                        ps_s0[:, 0:TB], AF.Identity, scale=DSC,
                        bias=qb_sc[:, qc:qc + 1]), inc="se")
        p.barrier()

    if dev:
        vmdump = nc.dram_tensor("vmdump", [128, (S // 128) * (EDIM + 1)], BF16,
                                kind="ExternalOutput").ap()
        odump_raw = nc.dram_tensor("odump_raw", [16, NT, 128, EDIM + 1], F32,
                                   kind="ExternalOutput").ap()
    vmap_holder = []
    with (
        nc.sbuf_tensor("kmap", [128, 2 * S if layer == 0 else S], BF16) as kmap,
        nc.sbuf_tensor("vmap", [128, (S // 128) * (EDIM + 1)], BF16) as vmap,
        nc.sbuf_tensor("masks_sb", [128, 2 * KB], F32) as masks,
        nc.sbuf_tensor("Esb", [128, 2 * KB], BF16) as Esb,
        nc.sbuf_tensor("etsb", [128, 2 * KB], BF16) as etsb,
        nc.sbuf_tensor("aseg", [1, S], F16) as aseg,
        nc.sbuf_tensor("ones16", [1, 128], F16) as ones16,
    ):
        p.step("vector", lambda E: E.memset(ones16[:, :], 1.0), inc="ve")
        p.step("sync", lambda E: E.dma_start(
            out=masks[:, :],
            in_=bass.AP(masks_in.tensor, 0,
                        [[KB, 128], [128 * KB, 2], [1, KB]])),
            inc="ld", inc_by=16)
        p.wait("vector", "ld")

        vmap_holder.append(vmap)
        ncomp = 2 if layer == 0 else 1
        state = {"t": 0, "se_exp": {}, "se_evac": {}, "pe_tr": {},
                 "pe_av": 0}

        def flash_tile(mapi, comp, j, qcb, kmap_off, ocol, state):
            nb = j // 2 + 1
            for kb in range(nb):
                t = state["t"]
                si = t % 2
                Sps = psS[si]
                Eps = psE[si]
                diag = (kb == nb - 1)
                C = _alibi_shift(mapi, kb) if layer == 1 else 0.0
                mold, mnew = (kb + 1) % 2, kb % 2
                p.wait("tensor", "se", state["se_exp"].get(t - 2, 0))
                for hh in range(2):
                    last = (hh == 1 and layer == 0)
                    p.step("tensor",
                           lambda E, j=j, kb=kb, hh=hh, Sps=Sps, qcb=qcb,
                           kmap_off=kmap_off: E.matmul(
                               Sps[:, hh * 512:(hh + 1) * 512],
                               qall[:, qcb * TOK + j * 128:
                                    qcb * TOK + (j + 1) * 128],
                               kmap[:, kmap_off + kb * KB + hh * 512:
                                    kmap_off + kb * KB + (hh + 1) * 512],
                               start=True, stop=(layer == 0)),
                           inc="pe" if last else None)
                    if layer == 1:
                        p.step("tensor",
                               lambda E, kb=kb, hh=hh, Sps=Sps: E.matmul(
                                   Sps[:, hh * 512:(hh + 1) * 512],
                                   ones16[0:1, :],
                                   aseg[0:1, kb * KB + hh * 512:
                                        kb * KB + (hh + 1) * 512],
                                   start=False, stop=True),
                               inc="pe" if hh == 1 else None)
                pe_mm = p.cnt["pe"]

                p.wait("vector", "pe", pe_mm)
                if diag:
                    par = j % 2
                    p.step("vector", lambda E, Sps=Sps, par=par: E.tensor_add(
                        Sps[:, :], Sps[:, :],
                        masks[:, par * KB:(par + 1) * KB]), inc="ve")
                    p.wait("vector", "ve")
                p.step("vector", lambda E, Sps=Sps: E.reduce_max(
                    mst[:, 2:3], Sps[:, :], axis=AX.X, negate=True), inc="ve")
                p.wait("vector", "ve")
                if kb == 0:
                    if C != 0.0:
                        p.step("vector", lambda E, C=C, mnew=mnew:
                               E.tensor_scalar_add(
                                   mst[:, mnew:mnew + 1], mst[:, 2:3], -C),
                               inc="ve")
                    else:
                        p.step("vector", lambda E, mnew=mnew: E.tensor_copy(
                            mst[:, mnew:mnew + 1], mst[:, 2:3]), inc="ve")
                else:
                    p.step("vector",
                           lambda E, C=C, mold=mold, mnew=mnew:
                           E.scalar_tensor_tensor(
                               mst[:, mnew:mnew + 1], mst[:, 2:3], -C,
                               mst[:, mold:mold + 1], ALU.add, ALU.min),
                           inc="ve")
                if C != 0.0:
                    p.wait("vector", "ve")
                    p.step("vector", lambda E, C=C, mnew=mnew: E.tensor_scalar_add(
                        mst[:, 3:4], mst[:, mnew:mnew + 1], C), inc="ve")
                ebc = 3 if C != 0.0 else mnew
                ve_state = p.cnt["ve"]

                p.wait("scalar", "ve", ve_state)
                if kb > 0:
                    p.step("scalar",
                           lambda E, mold=mold, mnew=mnew: E.activation(
                               mst[:, 4:5], mst[:, mold:mold + 1], AF.Exp,
                               scale=-1.0, bias=mst[:, mnew:mnew + 1]),
                           inc="se")
                p.wait("scalar", "pe", state["pe_tr"].get(t - 2, 0))
                p.step("scalar", lambda E, Sps=Sps, si=si, ebc=ebc: E.activation(
                    Esb[:, si * KB:(si + 1) * KB], Sps[:, :], AF.Exp,
                    bias=mst[:, ebc:ebc + 1]), inc="se")
                se_exp = p.cnt["se"]
                state["se_exp"][t] = se_exp

                p.wait("vector", "se", se_exp)
                if kb > 0:
                    p.wait("vector", "pe", state["pe_av"])
                    p.step("vector", lambda E, ocol=ocol: E.tensor_scalar_mul(
                        ps_o[:, ocol:ocol + EDIM + 1],
                        ps_o[:, ocol:ocol + EDIM + 1],
                        mst[:, 4:5]), inc="ve")
                ve_resc = p.cnt["ve"]

                p.wait("tensor", "se", se_exp)
                p.wait("tensor", "se", state["se_evac"].get(t - 2, 0))
                for u in range(8):
                    p.step("tensor", lambda E, u=u, si=si, Eps=Eps: E.transpose(
                        Eps[:, u * 128:(u + 1) * 128],
                        Esb[:, si * KB + u * 128: si * KB + (u + 1) * 128],
                        ident[:, :]), inc="pe" if u == 7 else None)
                pe_tr = p.cnt["pe"]
                state["pe_tr"][t] = pe_tr

                p.wait("scalar", "pe", pe_tr)
                p.wait("scalar", "pe", state["pe_av"])
                p.step("scalar", lambda E, si=si, Eps=Eps: E.copy(
                    etsb[:, si * KB:(si + 1) * KB], Eps[:, :]), inc="se")
                se_evac = p.cnt["se"]
                state["se_evac"][t] = se_evac

                p.wait("tensor", "se", se_evac)
                p.wait("tensor", "ve", ve_resc)
                for u in range(8):
                    p.step("tensor",
                           lambda E, u=u, kb=kb, si=si, ocol=ocol: E.matmul(
                               ps_o[:, ocol:ocol + EDIM + 1],
                               etsb[:, si * KB + u * 128:
                                    si * KB + (u + 1) * 128],
                               vmap[:, (kb * 8 + u) * (EDIM + 1):
                                    (kb * 8 + u + 1) * (EDIM + 1)],
                               start=(kb == 0 and u == 0),
                               stop=(kb == nb - 1 and u == 7)),
                           inc="pe" if u == 7 else None)
                state["pe_av"] = p.cnt["pe"]
                state["t"] = t + 1

        nmaps = NUNIT
        for mi in range(nmaps):
            if layer == 0:
                kvg = mi // 2
                kf0 = kvg * 256
                vf0 = kvg * 256
                need_kv = (mi % 2 == 0)
            else:
                kf0 = mi * 128
                vf0 = mi * 128
                need_kv = True
            if need_kv:
                p.wait("sync", "pe")
                nk = 2 if layer == 0 else 1
                for cc in range(nk):
                    p.step("sync", lambda E, kf0=kf0, cc=cc: E.dma_start(
                        out=kmap[:, cc * S:(cc + 1) * S],
                        in_=bass.AP(k_dram, (kf0 + cc * 128) * S,
                                    [[S, 128], [1, S]])), inc="ld", inc_by=16)
                NSPL = 8
                csz = (S // 128) // NSPL
                for sp in range(NSPL):
                    p.step("sync", lambda E, vf0=vf0, sp=sp, csz=csz: E.dma_start(
                        out=bass.AP(vmap, sp * csz * (EDIM + 1),
                                    [[vmap.shape[1], 128],
                                     [EDIM + 1, csz], [1, EDIM]]),
                        in_=bass.AP(v_dram, vf0 + sp * csz * 128 * VF,
                                    [[VF, 128], [128 * VF, csz], [1, EDIM]])),
                        inc="ld", inc_by=16)
                kv_ok = p.cnt["ld"]
                p.wait("tensor", "ld", kv_ok)
                p.wait("vector", "ld")
                p.wait("vector", "pe")
                if layer == 1:
                    p.step("sync", lambda E, mi=mi: E.dma_start(
                        out=aseg[:, :], in_=alibi_in[mi:mi + 1, :]),
                        inc="ld", inc_by=16)
                    p.wait("tensor", "ld")
                p.step("vector", lambda E: E.memset(
                    bass.AP(vmap, EDIM, [[vmap.shape[1], 128],
                                         [EDIM + 1, S // 128]]), 1.0),
                    inc="ve")
                p.wait("tensor", "ve")
                if dev and layer == 1 and mi == NUNIT - 1:
                    p.wait("gpsimd", "ve")
                    p.step("gpsimd", lambda E: E.dma_start(
                        out=vmdump[:, :], in_=vmap[:, :]), inc="gd", inc_by=16)
                    p.wait("vector", "gd")

            if layer == 0:
                pass

            for j in range(NT):
                if layer == 0:
                    unit = mi
                    flash_tile(unit, 0, j, unit * 2 + 0, 0, 0, state)
                    p.wait("vector", "pe", state["pe_av"])
                    p.wait("vector", "st")
                    p.step("vector", lambda E: E.reciprocal(
                        mst[:, 8:9], ps_o[:, EDIM:EDIM + 1]), inc="ve")
                    p.wait("vector", "ve")
                    p.step("vector", lambda E: E.tensor_scalar_mul(
                        obuf[:, 0:EDIM], ps_o[:, 0:EDIM], mst[:, 8:9]),
                        inc="ve")
                    p.wait("tensor", "ve")
                    flash_tile(unit, 1, j, unit * 2 + 1, S, 512, state)
                    p.wait("vector", "pe", state["pe_av"])
                    p.step("vector", lambda E: E.reciprocal(
                        mst[:, 9:10], ps_o[:, 512 + EDIM:513 + EDIM]),
                        inc="ve")
                    p.wait("vector", "ve")
                    p.step("vector", lambda E: E.tensor_mul(
                        mst[:, 10:11], lamn[:, :], mst[:, 9:10]), inc="ve")
                    p.wait("vector", "ve")
                    p.step("vector", lambda E: E.scalar_tensor_tensor(
                        obuf[:, 0:EDIM], ps_o[:, 512:512 + EDIM],
                        mst[:, 10:11], obuf[:, 0:EDIM],
                        ALU.mult, ALU.add), inc="ve")
                    ve_m = p.cnt["ve"]
                    p.wait("scalar", "ve", ve_m)
                    p.step("scalar", lambda E: E.activation(
                        obufb[:, 0:EDIM], obuf[:, 0:EDIM], AF.Square,
                        accum_out=mst[:, 11:12]), inc="se")
                    p.wait("vector", "se")
                    p.step("vector", lambda E: E.tensor_scalar(
                        mst[:, 11:12], mst[:, 11:12], 1.0 / EDIM, 1e-5,
                        ALU.mult, ALU.add), inc="ve")
                    p.wait("scalar", "ve")
                    p.step("scalar", lambda E: E.activation(
                        mst[:, 12:13], mst[:, 11:12], AF.Sqrt), inc="se")
                    p.wait("vector", "se")
                    p.step("vector", lambda E: E.reciprocal(
                        mst[:, 13:14], mst[:, 12:13]), inc="ve")
                    p.wait("vector", "ve")
                    p.step("vector", lambda E: E.scalar_tensor_tensor(
                        obufb[:, 0:EDIM], obuf[:, 0:EDIM], mst[:, 13:14],
                        subwb[:, :], ALU.mult, ALU.mult), inc="ve")
                    ve_o = p.cnt["ve"]
                    p.wait("sync", "ve", ve_o)
                    p.step("sync", lambda E, unit=unit, j=j: E.dma_start(
                        out=bass.AP(o_dram, j * 128 * D + unit * 256,
                                    [[D, 128], [1, 256]]),
                        in_=obufb[:, 0:EDIM]), inc="st", inc_by=16)
                    p.wait("tensor", "st")
                else:
                    flash_tile(mi, 0, j, mi, 0, 0, state)
                    p.wait("vector", "pe", state["pe_av"])
                    p.wait("vector", "st")
                    if dev:
                        p.step("vector", lambda E: E.tensor_copy(
                            obuf[:, 0:EDIM + 1], ps_o[:, 0:EDIM + 1]), inc="ve")
                        p.wait("sync", "ve")
                        p.step("sync", lambda E, mi=mi, j=j: E.dma_start(
                            out=odump_raw[mi, j, :, :], in_=obuf[:, 0:EDIM + 1]),
                            inc="st", inc_by=16)
                        p.wait("vector", "st")
                    p.step("vector", lambda E: E.reciprocal(
                        mst[:, 7:8], ps_o[:, EDIM:EDIM + 1]), inc="ve")
                    p.wait("vector", "ve")
                    p.step("vector", lambda E: E.tensor_scalar_mul(
                        obufb[:, 0:EDIM], ps_o[:, 0:EDIM], mst[:, 7:8]),
                        inc="ve")
                    ve_o = p.cnt["ve"]
                    p.wait("sync", "ve", ve_o)
                    p.step("sync", lambda E, mi=mi, j=j: E.dma_start(
                        out=bass.AP(o_dram, j * 128 * D + mi * 128,
                                    [[D, 128], [1, 128]]),
                        in_=obufb[:, 0:EDIM]), inc="st", inc_by=16)
                    p.wait("tensor", "st")
        p.barrier()

    with (
        nc.sbuf_tensor("wo_sb", [128, 16 * 2048], BF16) as wo_sb,
        nc.sbuf_tensor("o_tile", [128, D], BF16) as o_tile,
        nc.sbuf_tensor("o_fm", [128, 16 * 128], BF16) as o_fm,
        nc.sbuf_tensor("xres", [128, D], F32) as xres,
        nc.sbuf_tensor("ysb", [128, D], F32) as ysb,
        nc.sbuf_tensor("sq", [128, D], BF16) as sq,
        nc.sbuf_tensor("xo_sb", [128, D], F32) as xo_sb,
    ):
        p.step("gpsimd", lambda E: E.dma_start(
            out=wo_sb[:, :],
            in_=bass.AP(w_o.tensor, 0,
                        [[D, 128], [128 * D, 16], [1, D]])), inc="gd", inc_by=16)
        p.wait("tensor", "gd")
        for j in range(NT):
            p.wait("sync", "pe")
            p.wait("sync", "ve")
            p.wait("sync", "se")
            p.step("sync", lambda E, j=j: E.dma_start(
                out=o_tile[:, :],
                in_=bass.AP(o_dram, j * 128 * D, [[D, 128], [1, D]])),
                inc="ld", inc_by=16)
            p.step("sync", lambda E, j=j: E.dma_start(
                out=xres[:, :],
                in_=bass.AP(xq.tensor, j * 128 * D,
                            [[D, 128], [1, D]])), inc="ld", inc_by=16)
            p.wait("tensor", "ld")
            p.wait("vector", "ld")
            p.wait("vector", "se")
            for dg in range(4):
                p.step("vector", lambda E, dg=dg: E.tensor_add(
                    xres[:, dg * 512:(dg + 1) * 512],
                    xres[:, dg * 512:(dg + 1) * 512],
                    bob[:, dg * 512:(dg + 1) * 512]), inc="ve")
            p.wait("vector", "ve")
            emit_tm2fm(p, o_tile, o_fm, ident, 1, ps_tr[:, 0:512], tcols=128,
                       src_stride=D)
            for dg in range(4):
                p.wait("tensor", "ve")
                for ec in range(16):
                    p.step("tensor", lambda E, ec=ec, dg=dg: E.matmul(
                        ps_s0[:, 0:512],
                        o_fm[:, ec * 128:(ec + 1) * 128],
                        wo_sb[:, ec * 2048 + dg * 512: ec * 2048 + (dg + 1) * 512],
                        start=(ec == 0), stop=(ec == 15)),
                        inc="pe" if ec == 15 else None)
                p.wait("vector", "pe")
                p.step("vector", lambda E, dg=dg: E.tensor_add(
                    ysb[:, dg * 512:(dg + 1) * 512], ps_s0[:, 0:512],
                    xres[:, dg * 512:(dg + 1) * 512]), inc="ve")
            p.wait("scalar", "st")
            emit_ln_epilogue(p, ysb, sq, xo_sb, mst)
            p.wait("sync", "se")
            p.step("sync", lambda E, j=j: E.dma_start(
                out=bass.AP(xout.tensor, j * 128 * D, [[D, 128], [1, D]]),
                in_=xo_sb[:, :]), inc="st", inc_by=16)
    if dev:
        kdump = nc.dram_tensor("kdump", [KF, S], BF16, kind="ExternalOutput").ap()
        vdump = nc.dram_tensor("vdump", [S, VF], BF16, kind="ExternalOutput").ap()
        odump = nc.dram_tensor("odump", [TOK, D], BF16, kind="ExternalOutput").ap()
        qdump = nc.dram_tensor("qdump", [128, 16 * TOK], BF16,
                               kind="ExternalOutput").ap()
        p.barrier(("gpsimd",))
        p.step("gpsimd", lambda E: E.dma_start(out=kdump[:, :], in_=k_dram.ap()),
               inc="gd", inc_by=16)
        p.step("gpsimd", lambda E: E.dma_start(out=vdump[:, :], in_=v_dram.ap()),
               inc="gd", inc_by=16)
        p.step("gpsimd", lambda E: E.dma_start(out=odump[:, :], in_=o_dram.ap()),
               inc="gd", inc_by=16)
        p.step("gpsimd", lambda E: E.dma_start(out=qdump[:, :], in_=qall[:, :]),
               inc="gd", inc_by=16)
        p.wait("gpsimd", "gd")
    p.wait("gpsimd", "st")
    return p.finalize()



def emit_swiglu_block(p, nc, bufs, x_fm, macc, w_up1, w_up3, w_down, nh,
                      up_off1=0, up_off3=0, up_base=0, down_base=0,
                      up_stride=None, down_stride=D, acc_scale=None):
    wbuf, hsil, hsup, wdg = bufs["wbuf"], bufs["hsil"], bufs["hsup"], bufs["wdg"]
    ps_a, ps_b, ps_c = bufs["ps_a"], bufs["ps_b"], bufs["ps_c"]
    if up_stride is None:
        up_stride = w_up1.shape[-1]
    TBF = 512
    NH_CH = nh // 128
    st_marks = bufs.setdefault("marks", {})
    for hs in range((NH_CH + 15) // 16):
        nch = min(16, NH_CH - hs * 16)
        for hc in range(nch):
            h = hs * 16 + hc
            half = h % 2
            p.wait("gpsimd", "pe")
            for q4 in range(4):
                p.step("gpsimd", lambda E, h=h, half=half, w=w_up1,
                       o=up_base + up_off1, us=up_stride, q4=q4:
                       E.dma_start(
                           out=wbuf[:, half * 4096 + q4 * 512:
                                half * 4096 + (q4 + 1) * 512],
                           in_=bass.AP(w.tensor, o + h * 128 + q4 * 4 * 128 * us,
                                       [[us, 128], [128 * us, 4], [1, 128]])),
                       inc="gd", inc_by=16)
                p.step("gpsimd", lambda E, h=h, half=half, w=w_up3,
                       o=up_base + up_off3, us=up_stride, q4=q4:
                       E.dma_start(
                           out=wbuf[:, half * 4096 + 2048 + q4 * 512:
                                half * 4096 + 2048 + (q4 + 1) * 512],
                           in_=bass.AP(w.tensor, o + h * 128 + q4 * 4 * 128 * us,
                                       [[us, 128], [128 * us, 4], [1, 128]])),
                       inc="gd", inc_by=16)
            p.wait("tensor", "gd")
            p.wait("tensor", "se")
            p.wait("tensor", "ve")
            for c in range(16):
                p.step("tensor", lambda E, c=c, half=half: E.matmul(
                    ps_a[:, 0:TBF],
                    wbuf[:, half * 4096 + c * 128:half * 4096 + (c + 1) * 128],
                    x_fm[:, c * TBF:(c + 1) * TBF],
                    start=(c == 0), stop=(c == 15)), inc=None)
            for c in range(16):
                p.step("tensor", lambda E, c=c, half=half: E.matmul(
                    ps_b[:, 0:TBF],
                    wbuf[:, half * 4096 + 2048 + c * 128:
                         half * 4096 + 2048 + (c + 1) * 128],
                    x_fm[:, c * TBF:(c + 1) * TBF],
                    start=(c == 0), stop=(c == 15)),
                    inc="pe" if c == 15 else None)
            st_marks[("u", half)] = p.cnt["pe"]
            p.wait("scalar", "pe")
            p.wait("scalar", "ve")
            p.step("scalar", lambda E: E.activation(
                hsil[:, 0:TBF], ps_a[:, 0:TBF], AF.Silu), inc="se")
            p.wait("vector", "se")
            p.wait("vector", "pe")
            p.step("vector", lambda E, hc=hc: E.tensor_mul(
                hsup[:, hc * TBF:(hc + 1) * TBF], hsil[:, 0:TBF],
                ps_b[:, 0:TBF]), inc="ve")
        hs_ready = p.cnt["ve"]
        for dg in range(4):
            halfd = dg % 2
            p.wait("gpsimd", "pe")
            nspl = (nch + 3) // 4
            for q4 in range(nspl):
                nc4 = min(4, nch - q4 * 4)
                p.step("gpsimd", lambda E, hs=hs, dg=dg, halfd=halfd, w=w_down,
                       nc4=nc4, q4=q4, db=down_base, ds=down_stride:
                       E.dma_start(
                           out=wdg[:, halfd * 8192 + (q4 * 4) * 512:
                                halfd * 8192 + (q4 * 4 + nc4) * 512],
                           in_=bass.AP(w.tensor,
                                       db + (hs * 16 + q4 * 4) * 128 * ds + dg * 512,
                                       [[ds, 128], [128 * ds, nc4], [1, 512]])),
                       inc="gd", inc_by=16)
            p.wait("tensor", "gd")
            p.wait("tensor", "ve", hs_ready)
            for tsub in range(4):
                p.wait("tensor", "ve")
                for hc in range(nch):
                    p.step("tensor",
                           lambda E, hc=hc, tsub=tsub, halfd=halfd, nch=nch:
                           E.matmul(
                               ps_c[:, 0:512],
                               hsup[:, hc * TBF + tsub * 128:
                                    hc * TBF + (tsub + 1) * 128],
                               wdg[:, halfd * 8192 + hc * 512:
                                    halfd * 8192 + (hc + 1) * 512],
                               start=(hc == 0), stop=(hc == nch - 1)),
                           inc="pe" if hc == nch - 1 else None)
                if tsub == 3:
                    st_marks[("d", halfd)] = p.cnt["pe"]
                p.wait("vector", "pe")
                if acc_scale is None:
                    p.step("vector", lambda E, tsub=tsub, dg=dg: E.tensor_add(
                        macc[:, tsub * D + dg * 512:tsub * D + (dg + 1) * 512],
                        macc[:, tsub * D + dg * 512:tsub * D + (dg + 1) * 512],
                        ps_c[:, 0:512]), inc="ve")
                else:
                    p.step("vector",
                           lambda E, tsub=tsub, dg=dg, sc=acc_scale(tsub):
                           E.scalar_tensor_tensor(
                               macc[:, tsub * D + dg * 512:
                                    tsub * D + (dg + 1) * 512],
                               ps_c[:, 0:512], sc,
                               macc[:, tsub * D + dg * 512:
                                    tsub * D + (dg + 1) * 512],
                               ALU.mult, ALU.add), inc="ve")


def emit_swiglu_driver(p, nc, xq, xout, ident, mst, psl, per_tb_fn,
                       macc_dump=None):
    ps_a, ps_b, ps_c, ps_tr = psl
    NSUB = 4
    TBF = 512
    with (
        nc.sbuf_tensor("f_x_tm", [128, NSUB * D], F32) as x_tm,
        nc.sbuf_tensor("f_x_tmb", [128, NSUB * D], BF16) as x_tmb,
        nc.sbuf_tensor("f_x_fm", [128, 16 * TBF], BF16) as x_fm,
        nc.sbuf_tensor("f_wbuf", [128, 2 * 2 * 2048], BF16) as wbuf,
        nc.sbuf_tensor("f_hsil", [128, TBF], F32) as hsil,
        nc.sbuf_tensor("f_hsup", [128, 16 * TBF], BF16) as hsup,
        nc.sbuf_tensor("f_wdg", [128, 2 * 16 * 512], BF16) as wdg,
        nc.sbuf_tensor("f_macc", [128, NSUB * D], F32) as macc,
        nc.sbuf_tensor("f_ysb", [128, D], F32) as ysb,
        nc.sbuf_tensor("f_sq", [128, D], BF16) as sq,
        nc.sbuf_tensor("f_xo", [128, D], F32) as xo_sb,
    ):
        bufs = {"wbuf": wbuf, "hsil": hsil, "hsup": hsup, "wdg": wdg,
                "ps_a": ps_a, "ps_b": ps_b, "ps_c": ps_c, "x_tm": x_tm,
                "x_tmb": x_tmb}
        for tb in range(4):
            p.wait("sync", "ve")
            p.wait("sync", "pe")
            p.wait("sync", "se")
            p.step("sync", lambda E, tb=tb: E.dma_start(
                out=x_tm[:, :],
                in_=bass.AP(xq.tensor, tb * TBF * D,
                            [[D, 128], [128 * D, NSUB], [1, D]])),
                inc="ld", inc_by=16)
            p.wait("vector", "ld")
            p.wait("vector", "pe")
            for st in range(NSUB):
                p.step("vector", lambda E, st=st: E.tensor_copy(
                    x_tmb[:, st * D:(st + 1) * D],
                    x_tm[:, st * D:(st + 1) * D]), inc="ve")
            p.wait("tensor", "ve")
            emit_tm2fm(p, x_tmb, x_fm, ident, NSUB, ps_tr[:, 0:512], tcols=TBF)
            p.wait("vector", "se")
            p.step("vector", lambda E: E.memset(macc[:, :], 0.0), inc="ve")
            p.wait("tensor", "ve")
            p.barrier()
            per_tb_fn(tb, x_fm, macc, bufs)
            p.barrier()
            if macc_dump is not None:
                p.step("sync", lambda E, tb=tb: E.dma_start(
                    out=macc_dump[tb, :, :], in_=macc[:, :]),
                    inc="st", inc_by=16)
                p.wait("vector", "st")
                p.wait("scalar", "st")
            for tsub in range(NSUB):
                p.wait("vector", "pe")
                p.wait("vector", "se")
                for dgg in range(4):
                    p.step("vector", lambda E, tsub=tsub, dgg=dgg: E.tensor_add(
                        ysb[:, dgg * 512:(dgg + 1) * 512],
                        macc[:, tsub * D + dgg * 512:tsub * D + (dgg + 1) * 512],
                        x_tm[:, tsub * D + dgg * 512:tsub * D + (dgg + 1) * 512]),
                        inc="ve")
                p.wait("scalar", "st")
                p.barrier(("vector", "scalar"))
                emit_ln_epilogue(p, ysb, sq, xo_sb, mst)
                p.wait("sync", "se")
                p.step("sync", lambda E, tb=tb, tsub=tsub: E.dma_start(
                    out=bass.AP(xout.tensor, (tb * NSUB + tsub) * 128 * D,
                                [[D, 128], [1, D]]),
                    in_=xo_sb[:, :]), inc="st", inc_by=16)
                p.barrier(("vector", "scalar"))


def build_ffn_nc(dev=False):
    nc = bass.Bass("TRN2", target_bir_lowering=False, debug=False)
    p = Prog(nc)

    xq = nc.dram_tensor("xq", [TOK, D], F32, kind="ExternalInput").ap()
    ident_in = nc.dram_tensor("ident", [128, 128], F32, kind="ExternalInput").ap()
    w1t = nc.dram_tensor("w1t", [D, 8192], F32, kind="ExternalInput").ap()
    w3t = nc.dram_tensor("w3t", [D, 8192], F32, kind="ExternalInput").ap()
    w2t = nc.dram_tensor("w2t", [8192, D], F32, kind="ExternalInput").ap()
    h128_in = nc.dram_tensor("h128", [128, 128], F32, kind="ExternalInput").ap()
    h64_in = nc.dram_tensor("h64", [64, 64], F32, kind="ExternalInput").ap()
    xout = nc.dram_tensor("xout", [TOK, D], F32, kind="ExternalOutput").ap()

    t1_dram = nc.dram_tensor("t1_scr", [128, 64 * D], BF16, kind="Internal")
    w2h_dram = nc.dram_tensor("w2h_scr", [8192, D], BF16, kind="Internal")

    ident = nc.alloc_sbuf_tensor("identb", [128, 128], BF16)
    h128b = nc.alloc_sbuf_tensor("h128b", [128, 128], BF16)
    h64b = nc.alloc_sbuf_tensor("h64b", [64, 64], BF16)
    mst = nc.alloc_sbuf_tensor("mst", [128, 16], F32)

    ps_a = nc.alloc_psum_tensor("ps_a", [128, 512], F32)
    ps_b = nc.alloc_psum_tensor("ps_b", [128, 512], F32)
    ps_c = nc.alloc_psum_tensor("ps_c", [128, 512], F32)
    ps_tr = nc.alloc_psum_tensor("ps_tr", [128, 512], BF16)

    p.step("gpsimd", lambda E: E.dma_start(out=ident[:, :], in_=ident_in[:, :]),
           inc="gd", inc_by=16)
    p.step("gpsimd", lambda E: E.dma_start(out=h128b[:, :], in_=h128_in[:, :]),
           inc="gd", inc_by=16)
    p.step("gpsimd", lambda E: E.dma_start(out=h64b[:, :], in_=h64_in[:, :]),
           inc="gd", inc_by=16)
    p.wait("tensor", "gd")

    emit_w2h_build(p, nc, w2t, t1_dram, w2h_dram, h128b, h64b, ps_a, ps_b)

    def per_tb(tb, x_fm, macc, bufs):
        emit_swiglu_block(p, nc, bufs, x_fm, macc, w1t, w3t, w2h_dram.ap(),
                          8192)

    emit_swiglu_driver(p, nc, xq, xout, ident, mst,
                       [ps_a, ps_b, ps_c, ps_tr], per_tb)
    if dev:
        w2hd = nc.dram_tensor("w2hd", [8192, D], BF16, kind="ExternalOutput").ap()
        p.barrier(("gpsimd",))
        p.step("gpsimd", lambda E: E.dma_start(out=w2hd[:, :], in_=w2h_dram.ap()),
               inc="gd", inc_by=16)
        p.wait("gpsimd", "gd")
    p.wait("gpsimd", "st")
    return p.finalize()


def emit_w2h_build(p, nc, w2t, t1_dram, w2h_dram, h128b, h64b, ps_a, ps_b,
                   serial=True):
    FWHT_SC = float(1.0 / np.sqrt(8192.0))
    with (
        nc.sbuf_tensor("wh_ld", [128, D], BF16) as wld,
        nc.sbuf_tensor("wh_ev", [128, D], BF16) as wev,
        nc.psum_tensor("wh_ps", [128, D], mybir.dt.float32) as ps_w,
    ):
        def stage(ldq, load_fn, mm_fn, ev_fn, store_fn):
            lsem = "ld" if ldq == "sync" else "gd"
            p.step(ldq, load_fn, inc=lsem, inc_by=16)
            p.wait("tensor", lsem)
            for f in mm_fn:
                p.step("tensor", f, inc="pe")
            p.wait("scalar", "pe")
            p.step("scalar", ev_fn, inc="se")
            p.wait("sync", "se")
            p.step("sync", store_fn, inc="st", inc_by=16)
            p.barrier(("sync", "gpsimd", "tensor", "scalar"))

        for i in range(64):
            mms = [
                (lambda E, db=db: E.matmul(
                    ps_w[:, db * 512:(db + 1) * 512],
                    h128b[:, :], wld[:, db * 512:(db + 1) * 512],
                    start=True, stop=True))
                for db in range(4)]
            stage("gpsimd",
                  lambda E, i=i: E.dma_start(
                      out=wld[:, :],
                      in_=bass.AP(w2t.tensor, i * 128 * D, [[D, 128], [1, D]])),
                  mms,
                  lambda E: E.activation(wev[:, :], ps_w[:, :],
                                         AF.Identity, scale=FWHT_SC),
                  lambda E, i=i: E.dma_start(
                      out=bass.AP(t1_dram, i * D, [[64 * D, 128], [1, D]]),
                      in_=wev[:, :]))
        for jp in range(128):
            mms = [
                (lambda E, db=db: E.matmul(
                    ps_w[0:64, db * 512:(db + 1) * 512],
                    h64b[:, :], wld[0:64, db * 512:(db + 1) * 512],
                    start=True, stop=True))
                for db in range(4)]
            stage("sync",
                  lambda E, jp=jp: E.dma_start(
                      out=wld[0:64, :],
                      in_=bass.AP(t1_dram, jp * 64 * D, [[D, 64], [1, D]])),
                  mms,
                  lambda E: E.activation(wev[0:64, :], ps_w[0:64, :],
                                         AF.Identity),
                  lambda E, jp=jp: E.dma_start(
                      out=bass.AP(w2h_dram, jp * D, [[128 * D, 64], [1, D]]),
                      in_=wev[0:64, :]))
        p.barrier()



def build_moe_nc(dev=False):
    nc = bass.Bass("TRN2", target_bir_lowering=False, debug=False)
    p = Prog(nc)

    xq = nc.dram_tensor("xq", [TOK, D], F32, kind="ExternalInput").ap()
    ident_in = nc.dram_tensor("ident", [128, 128], F32, kind="ExternalInput").ap()
    sw1t = nc.dram_tensor("sw1t", [D, 8192], F32, kind="ExternalInput").ap()
    sw2t = nc.dram_tensor("sw2t", [D, 8192], F32, kind="ExternalInput").ap()
    sw3t = nc.dram_tensor("sw3t", [8192, D], F32, kind="ExternalInput").ap()
    w12t = nc.dram_tensor("w12t", [8, D, 2048], F32, kind="ExternalInput").ap()
    w3et = nc.dram_tensor("w3et", [8, 1024, D], F32, kind="ExternalInput").ap()
    rw_in = nc.dram_tensor("rw_lay", [128, 128], F32, kind="ExternalInput").ap()
    eb_in = nc.dram_tensor("ebias", [1, 8], F32, kind="ExternalInput").ap()
    xout = nc.dram_tensor("xout", [TOK, D], F32, kind="ExternalOutput").ap()
    if dev:
        combdump = nc.dram_tensor("combdump", [4, 128, 32], F32,
                                  kind="ExternalOutput").ap()
        maccdump = nc.dram_tensor("maccdump", [4, 128, 4 * D], F32,
                                  kind="ExternalOutput").ap()

    ident = nc.alloc_sbuf_tensor("identb", [128, 128], BF16)
    ident32 = nc.alloc_sbuf_tensor("ident32", [128, 128], F32)
    mst = nc.alloc_sbuf_tensor("mst", [128, 16], F32)
    rw_sb = nc.alloc_sbuf_tensor("rw_sb", [128, 128], F32)
    xfm32 = nc.alloc_sbuf_tensor("xfm32", [128, 16 * 128], F32)
    ebb = nc.alloc_sbuf_tensor("ebb", [128, 8], F32)
    ebr = nc.alloc_sbuf_tensor("ebr", [1, 8], F32)
    ones = nc.alloc_sbuf_tensor("ones", [1, 128], F32)
    scor = nc.alloc_sbuf_tensor("scor", [128, 8 * 8], F32)
    comb = nc.alloc_sbuf_tensor("comb", [128, 4 * 8], F32)

    ps_a = nc.alloc_psum_tensor("ps_a", [128, 512], F32)
    ps_b = nc.alloc_psum_tensor("ps_b", [128, 512], F32)
    ps_c = nc.alloc_psum_tensor("ps_c", [128, 512], F32)
    ps_tr = nc.alloc_psum_tensor("ps_tr", [128, 512], BF16)

    p.step("gpsimd", lambda E: E.dma_start(out=ident[:, :], in_=ident_in[:, :]),
           inc="gd", inc_by=16)
    p.step("sync", lambda E: E.dma_start(out=rw_sb[:, :], in_=rw_in[:, :]),
           inc="ld", inc_by=16)
    p.step("sync", lambda E: E.dma_start(out=ident32[:, :], in_=ident_in[:, :]),
           inc="ld", inc_by=16)
    p.step("sync", lambda E: E.dma_start(out=ebr[:, :], in_=eb_in[:, :]),
           inc="ld", inc_by=16)
    p.step("vector", lambda E: E.memset(ones[:, :], 1.0), inc="ve")
    p.wait("tensor", "gd")
    p.wait("tensor", "ld")
    p.wait("tensor", "ve")
    p.step("tensor", lambda E: E.matmul(
        ps_a[:, 0:8], ones[0:1, 0:128], ebr[0:1, 0:8],
        start=True, stop=True), inc="pe")
    p.wait("vector", "pe")
    p.step("vector", lambda E: E.tensor_copy(ebb[:, :], ps_a[:, 0:8]), inc="ve")
    p.wait("tensor", "ve")

    def vstep(f):
        p.step("vector", f, inc="ve")
        p.wait("vector", "ve")

    def per_tb(tb, x_fm, macc, bufs):
        for tsub in range(4):
            for c0 in range(0, 16, 4):
                p.wait("tensor", "ve")
                for u in range(4):
                    p.step("tensor", lambda E, c=c0 + u, u=u, tsub=tsub:
                           E.transpose(
                               ps_a[:, u * 128:(u + 1) * 128],
                               bufs["x_tm"][:, tsub * D + c * 128:
                                            tsub * D + (c + 1) * 128],
                               ident32[:, :]),
                           inc="pe" if u == 3 else None)
                p.wait("vector", "pe")
                p.step("vector", lambda E, c0=c0: E.tensor_copy(
                    xfm32[:, c0 * 128:(c0 + 4) * 128], ps_a[:, 0:512]),
                    inc="ve")
            p.wait("tensor", "ve")
            p.wait("tensor", "se")
            for c in range(16):
                p.step("tensor", lambda E, c=c: E.matmul(
                    ps_b[:, 0:8],
                    xfm32[:, c * 128:(c + 1) * 128],
                    rw_sb[:, c * 8:(c + 1) * 8],
                    start=(c == 0), stop=(c == 15)),
                    inc="pe" if c == 15 else None)
            p.wait("scalar", "pe")
            p.wait("scalar", "ve")
            p.step("scalar", lambda E: E.activation(
                scor[:, 0:8], ps_b[:, 0:8], AF.Sigmoid), inc="se")
            p.wait("vector", "se")
            vstep(lambda E: E.tensor_add(scor[:, 8:16], scor[:, 0:8], ebb[:, :]))
            vstep(lambda E: E.reduce_max(mst[:, 8:9], scor[:, 8:16], axis=AX.X))
            vstep(lambda E: E.tensor_scalar(
                scor[:, 16:24], scor[:, 8:16], mst[:, 8:9], None, ALU.subtract))
            vstep(lambda E: E.tensor_single_scalar(
                scor[:, 16:24], scor[:, 16:24], 0.0, ALU.is_ge))
            vstep(lambda E: E.scalar_tensor_tensor(
                scor[:, 24:32], scor[:, 16:24], NEG, scor[:, 8:16],
                ALU.mult, ALU.add))
            vstep(lambda E: E.reduce_max(mst[:, 9:10], scor[:, 24:32], axis=AX.X))
            vstep(lambda E: E.tensor_scalar(
                scor[:, 32:40], scor[:, 8:16], mst[:, 9:10], None, ALU.subtract))
            vstep(lambda E: E.tensor_single_scalar(
                scor[:, 32:40], scor[:, 32:40], 0.0, ALU.is_ge))
            vstep(lambda E: E.tensor_mul(scor[:, 40:48], scor[:, 0:8],
                                         scor[:, 32:40]))
            vstep(lambda E: E.reduce_sum(mst[:, 10:11], scor[:, 40:48],
                                         axis=AX.X))
            vstep(lambda E: E.tensor_scalar_add(mst[:, 10:11], mst[:, 10:11],
                                                1e-9))
            vstep(lambda E: E.reciprocal(mst[:, 11:12], mst[:, 10:11]))
            vstep(lambda E, tsub=tsub: E.tensor_scalar_mul(
                comb[:, tsub * 8:(tsub + 1) * 8], scor[:, 40:48],
                mst[:, 11:12]))
        p.barrier()
        if dev:
            p.wait("sync", "ve")
            p.step("sync", lambda E, tb=tb: E.dma_start(
                out=combdump[tb, :, :], in_=comb[:, :]),
                inc="st", inc_by=16)
            p.wait("vector", "st")

        p.barrier()
        emit_swiglu_block(p, nc, bufs, x_fm, macc, sw1t, sw2t, sw3t, 8192)
        for e in range(8):
            p.barrier()
            emit_swiglu_block(
                p, nc, bufs, x_fm, macc, w12t, w12t, w3et, 1024,
                up_off1=0, up_off3=1024, up_base=e * D * 2048,
                down_base=e * 1024 * D, up_stride=2048, down_stride=D,
                acc_scale=lambda tsub, e=e: comb[:, tsub * 8 + e:tsub * 8 + e + 1])
        p.barrier()

    emit_swiglu_driver(p, nc, xq, xout, ident, mst,
                       [ps_a, ps_b, ps_c, ps_tr], per_tb,
                       macc_dump=(maccdump if dev else None))
    p.wait("gpsimd", "st")
    return p.finalize()


def moe_inputs(params, x_slice):
    rw = params["L1_ffn_router_w"]
    rw_lay = np.zeros((128, 128), np.float32)
    for c in range(16):
        rw_lay[:, c * 8:(c + 1) * 8] = rw[:, c * 128:(c + 1) * 128].T
    return {
        "xq": x_slice,
        "ident": np.eye(128, dtype=np.float32),
        "sw1t": np.ascontiguousarray(params["L1_ffn_sw1"].T),
        "sw2t": np.ascontiguousarray(params["L1_ffn_sw2"].T),
        "sw3t": np.ascontiguousarray(params["L1_ffn_sw3"].T),
        "w12t": np.ascontiguousarray(params["L1_ffn_w12"].transpose(0, 2, 1)),
        "w3et": np.ascontiguousarray(params["L1_ffn_w3"].transpose(0, 2, 1)),
        "rw_lay": rw_lay,
        "ebias": params["L1_ffn_expert_bias"].reshape(1, 8),
    }



def _own_rows(xb_full, g):
    return np.ascontiguousarray(
        xb_full.reshape(S // 128, 128, -1)[g::4].reshape(TOK, -1))


def _scatter_rows(dst, rows, g):
    dst.reshape(S // 128, 128, -1)[g::4] = rows.reshape(NT, 128, -1)


def attention_inputs(layer, x_full, params, g):
    ident = np.eye(128, dtype=np.float32)
    masks = _diag_masks(g)
    d = {
        "xb": x_full,
        "xq": _own_rows(x_full, g),
        "ident": ident,
        "masks": masks,
    }
    if layer == 0:
        wqkv = params["L0_attn_Wqkv_w"]
        d["wqkv_t"] = np.ascontiguousarray(wqkv.T)
        d["wout_t"] = np.ascontiguousarray(params["L0_attn_out_w"].T)
        bq = params["L0_attn_Wqkv_b"]
        bias = np.zeros((128, 32), np.float32)
        bias[:, 0:16] = bq[0:2048].reshape(16, 128).T
        bias[:, 16:24] = bq[2048:3072].reshape(8, 128).T
        d["bqkv"] = bias
        bvr = np.zeros((1, D), np.float32)
        bvr[0, 0:1024] = bq[3072:4096]
        d["bv_row"] = bvr
        d["bout"] = params["L0_attn_out_b"].reshape(1, D)
        d["lamv"] = np.concatenate(
            [params["L0_attn_lambda_q1"], params["L0_attn_lambda_k1"],
             params["L0_attn_lambda_q2"], params["L0_attn_lambda_k2"]]
        ).astype(np.float32).reshape(1, 512)
        d["subw"] = params["L0_attn_subln_w"].reshape(1, 256)
    else:
        d["wq"] = params["L1_attn_Wq"]
        d["wk"] = params["L1_attn_Wk"]
        d["wv"] = params["L1_attn_Wv"]
        d["wo"] = params["L1_attn_Wo"]
        bias = np.zeros((128, 32), np.float32)
        bias[:, 0:16] = params["L1_attn_bq"].reshape(16, 128).T
        bias[:, 16:32] = params["L1_attn_bk"].reshape(16, 128).T
        d["bqkv"] = bias
        d["bv_row"] = params["L1_attn_bv"].reshape(1, D)
        d["bout"] = params["L1_attn_bo"].reshape(1, D)
        d["alibi_f"] = _alibi_rel()
    return d


_FFN_INPUTS_CACHE = {}


def ffn_inputs(params, x_slice):
    return {
        "xq": x_slice,
        "ident": np.eye(128, dtype=np.float32),
        "w1t": np.ascontiguousarray(params["L0_ffn_W1"].T),
        "w3t": np.ascontiguousarray(params["L0_ffn_W3"].T),
        "w2t": np.ascontiguousarray(params["L0_ffn_W2"].T),
        "h128": _sylvester(128), "h64": _sylvester(64),
    }


def _gather_full(shards):
    out = np.empty((B, S, D), np.float32)
    for c in range(8):
        b, g = c // 4, c % 4
        out[b].reshape(S // 128, 128, D)[g::4] = shards[c].reshape(NT, 128, D)
    return out


def kernel(**inputs):
    x0 = np.ascontiguousarray(inputs["L0_attn_hidden_states"], dtype=np.float32)
    params = inputs

    nc = build_attention_nc(0)
    im = []
    for c in range(8):
        b, g = c // 4, c % 4
        im.append(attention_inputs(0, np.ascontiguousarray(x0[b]), params, g))
    res = run_bass_kernel_spmd(nc, im, core_ids=list(range(8)))
    x1_own = [np.asarray(res.results[c]["xout"]) for c in range(8)]

    nc = build_ffn_nc()
    im = []
    fw = {
        "ident": np.eye(128, dtype=np.float32),
        "w1t": np.ascontiguousarray(params["L0_ffn_W1"].T),
        "w3t": np.ascontiguousarray(params["L0_ffn_W3"].T),
        "w2t": np.ascontiguousarray(params["L0_ffn_W2"].T),
        "h128": _sylvester(128), "h64": _sylvester(64),
    }
    for c in range(8):
        im.append(dict(fw, xq=x1_own[c]))
    res = run_bass_kernel_spmd(nc, im, core_ids=list(range(8)))
    x2_own = [np.asarray(res.results[c]["xout"]) for c in range(8)]
    x2_full = _gather_full(x2_own)

    nc = build_attention_nc(1)
    im = []
    for c in range(8):
        b, g = c // 4, c % 4
        d = attention_inputs(1, np.ascontiguousarray(x2_full[b]), params, g)
        d["xq"] = x2_own[c]
        im.append(d)
    res = run_bass_kernel_spmd(nc, im, core_ids=list(range(8)))
    x3_own = [np.asarray(res.results[c]["xout"]) for c in range(8)]

    nc = build_moe_nc()
    im = [moe_inputs(params, x3_own[c]) for c in range(8)]
    res = run_bass_kernel_spmd(nc, im, core_ids=list(range(8)))
    out_own = [np.asarray(res.results[c]["xout"]) for c in range(8)]
    return _gather_full(out_own)


# version 03
# speedup vs baseline: 1.0056x; 1.0056x over previous
import math
from collections import defaultdict

import numpy as np

import concourse.bass as bass
import concourse.mybir as mybir
from concourse.bass_utils import run_bass_kernel_spmd

F32 = mybir.dt.float32
F16 = mybir.dt.float16
BF16 = mybir.dt.bfloat16
AF = mybir.ActivationFunctionType
ALU = mybir.AluOpType
AX = mybir.AxisListType

B = 2
S = 8192
D = 2048
TOK = 2048
NT = TOK // 128
KB = 1024
NEG = -1.0e30
LAM_INIT = 0.8 - 0.6 * math.exp(-0.3 * 1)
DSC = 128.0 ** -0.5
LNEPS = 1e-5



def _sylvester(n):
    h = np.array([[1.0]], np.float32)
    while h.shape[0] < n:
        h = np.block([[h, h], [h, -h]]).astype(np.float32)
    return h


def _slopes16():
    start = 2 ** (-2 ** (-(math.log2(16) - 3)))
    return np.array([start * start ** i for i in range(16)], np.float64)


def _diag_masks(g):
    m = np.zeros((2, 128, KB), np.float32)
    for par in range(2):
        off = 128 * g + 512 * par
        for qi in range(128):
            m[par, qi, off + qi + 1:] = NEG
    return m


def _alibi_rel():
    sl = _slopes16()
    k = np.arange(S, dtype=np.float64)
    kmax = (k // KB) * KB + (KB - 1)
    a = -sl[:, None] * (kmax[None] - k[None])
    return np.ascontiguousarray(a.astype(np.float16))


def _alibi_shift(h, kb):
    sl = _slopes16()
    return float(-sl[h] * (S - 1 - (kb * KB + KB - 1)))



ENGS = ("sync", "gpsimd", "tensor", "vector", "scalar")

_UNIQ = [0]


def _uq(name):
    _UNIQ[0] += 1
    return f"{name}_u{_UNIQ[0]}"


class Prog:
    def __init__(self, nc):
        self.nc = nc
        self.qs = {e: [] for e in ENGS}
        self.sems = {}
        self.cnt = defaultdict(int)
        self._guards = []

    def sem(self, name):
        if name not in self.sems:
            g = self.nc.semaphore(name)
            self.sems[name] = g.__enter__()
            self._guards.append(g)
        return self.sems[name]

    def wait(self, eng, sem, val=None):
        if val is None:
            val = self.cnt[sem]
        if val <= 0:
            return
        s = self.sem(sem)
        self.qs[eng].append(lambda E, s=s, v=val: E.wait_ge(s, v))

    def step(self, eng, build_fn, inc=None, inc_by=1):
        if inc is not None:
            s = self.sem(inc)
            self.cnt[inc] += inc_by

            def gg(E, f=build_fn, s=s, k=inc_by):
                f(E).then_inc(s, k)
            self.qs[eng].append(gg)
            return self.cnt[inc]
        self.qs[eng].append(build_fn)
        return None

    def barrier(self, engines=ENGS):
        snap = dict(self.cnt)
        for e in engines:
            for s, v in snap.items():
                if v > 0:
                    self.wait(e, s, v)

    def finalize(self):
        with self.nc.Block() as block:
            @block.sync
            def _(E):
                for f in self.qs["sync"]:
                    f(E)

            @block.gpsimd
            def _(E):
                for f in self.qs["gpsimd"]:
                    f(E)

            @block.tensor
            def _(E):
                for f in self.qs["tensor"]:
                    f(E)

            @block.vector
            def _(E):
                for f in self.qs["vector"]:
                    f(E)

            @block.scalar
            def _(E):
                for f in self.qs["scalar"]:
                    f(E)
        return self.nc



def emit_row_broadcast(p, ps_f32, ones_b, row_ap, out_sb, width):
    for i in range(0, width, 512):
        w = min(512, width - i)
        p.wait("tensor", "ve")
        p.step("tensor", lambda E, i=i, w=w: E.matmul(
            ps_f32[:, 0:w], ones_b[0:1, 0:128], row_ap[0:1, i:i + w],
            start=True, stop=True), inc="pe")
        p.wait("vector", "pe")
        p.step("vector", lambda E, i=i, w=w: E.tensor_copy(
            out_sb[:, i:i + w], ps_f32[:, 0:w]), inc="ve")
    p.wait("tensor", "ve")


def emit_tm2fm(p, src_bf, x_fm, ident, nsub, ps_tr, tcols=None, src_stride=D):
    if tcols is None:
        tcols = nsub * 128
    for c in range(16):
        for st0 in range(0, nsub, 4):
            n = min(4, nsub - st0)
            p.wait("tensor", "ve")
            for u in range(n):
                p.step("tensor", lambda E, c=c, st=st0, u=u: E.transpose(
                    ps_tr[:, u * 128:(u + 1) * 128],
                    src_bf[:, (st + u) * src_stride + c * 128:
                           (st + u) * src_stride + c * 128 + 128],
                    ident[:, :]), inc="pe" if u == n - 1 else None)
            p.wait("vector", "pe")
            p.step("vector", lambda E, c=c, st=st0, n=n: E.tensor_copy(
                x_fm[:, c * tcols + st * 128: c * tcols + (st + n) * 128],
                ps_tr[:, 0:n * 128]), inc="ve")
    p.wait("tensor", "ve")


def emit_ln_epilogue(p, ysb, sq, xo_sb, mst, width=D):
    p.wait("vector", "ve")
    p.step("vector", lambda E: E.reduce_sum(
        mst[:, 0:1], ysb[:, 0:width], axis=AX.X), inc="ve")
    p.wait("vector", "ve")
    p.step("vector", lambda E: E.tensor_scalar_mul(
        mst[:, 1:2], mst[:, 0:1], -1.0 / width), inc="ve")
    p.wait("scalar", "ve")
    p.step("scalar", lambda E: E.activation(
        sq[:, 0:width], ysb[:, 0:width], AF.Square,
        bias=mst[:, 1:2], accum_out=mst[:, 2:3]), inc="se")
    p.wait("vector", "se")
    p.step("vector", lambda E: E.tensor_scalar(
        mst[:, 2:3], mst[:, 2:3], 1.0 / width, LNEPS,
        ALU.mult, ALU.add), inc="ve")
    p.wait("scalar", "ve")
    p.step("scalar", lambda E: E.activation(
        mst[:, 3:4], mst[:, 2:3], AF.Sqrt), inc="se")
    p.wait("vector", "se")
    p.step("vector", lambda E: E.reciprocal(mst[:, 4:5], mst[:, 3:4]), inc="ve")
    p.wait("vector", "ve")
    p.step("vector", lambda E: E.tensor_mul(
        mst[:, 1:2], mst[:, 1:2], mst[:, 4:5]), inc="ve")
    p.wait("scalar", "ve")
    p.step("scalar", lambda E: E.activation(
        xo_sb[:, 0:width], ysb[:, 0:width], AF.Identity,
        scale=mst[:, 4:5], bias=mst[:, 1:2]), inc="se")



def declare_attention_inputs(nc, layer, shared):
    a = {}
    a["xb"] = nc.dram_tensor("xb", [S, D], F32, kind="ExternalInput").ap()
    if "xq" in shared:
        a["xq"] = shared["xq"]
    else:
        a["xq"] = nc.dram_tensor("xq", [TOK, D], F32, kind="ExternalInput").ap()
    if "ident" in shared:
        a["ident_in"] = shared["ident"]
    else:
        a["ident_in"] = nc.dram_tensor("ident", [128, 128], F32,
                                       kind="ExternalInput").ap()
    a["masks_in"] = nc.dram_tensor("masks", [2, 128, KB], F32,
                                   kind="ExternalInput").ap()
    a["b_qkv"] = nc.dram_tensor("bqkv", [128, 32], F32, kind="ExternalInput").ap()
    a["bv_row"] = nc.dram_tensor("bv_row", [1, D], F32, kind="ExternalInput").ap()
    a["b_out"] = nc.dram_tensor("bout", [1, D], F32, kind="ExternalInput").ap()
    if layer == 0:
        w_all = nc.dram_tensor("wqkv_t", [D, 4096], F32, kind="ExternalInput").ap()
        a["w_q"] = a["w_k"] = a["w_v"] = w_all
        a["w_o"] = nc.dram_tensor("wout_t", [D, D], F32, kind="ExternalInput").ap()
        a["lam_in"] = nc.dram_tensor("lamv", [1, 512], F32,
                                     kind="ExternalInput").ap()
        a["subw_in"] = nc.dram_tensor("subw", [1, 256], F32,
                                      kind="ExternalInput").ap()
        a["alibi_in"] = None
    else:
        a["w_q"] = nc.dram_tensor("wq", [D, D], F32, kind="ExternalInput").ap()
        a["w_k"] = nc.dram_tensor("wk", [D, D], F32, kind="ExternalInput").ap()
        a["w_v"] = nc.dram_tensor("wv", [D, D], F32, kind="ExternalInput").ap()
        a["w_o"] = nc.dram_tensor("wo", [D, D], F32, kind="ExternalInput").ap()
        a["alibi_in"] = nc.dram_tensor("alibi_f", [16, S], F16,
                                       kind="ExternalInput").ap()
    return a


def build_attention_nc(layer, dev=False):
    nc = bass.Bass("TRN2", target_bir_lowering=False, debug=False)
    p = Prog(nc)
    aps = declare_attention_inputs(nc, layer, {})
    xout = nc.dram_tensor("xout", [TOK, D], F32, kind="ExternalOutput").ap()
    emit_attention(p, nc, layer, aps, xout, dev=dev)
    p.wait("gpsimd", "st")
    return p.finalize()


def emit_attention(p, nc, layer, aps, xout, dev=False):
    import contextlib
    es = contextlib.ExitStack()
    xb, xq = aps["xb"], aps["xq"]
    ident_in, masks_in = aps["ident_in"], aps["masks_in"]
    b_qkv, bv_row, b_out = aps["b_qkv"], aps["bv_row"], aps["b_out"]
    w_q, w_k, w_v, w_o = aps["w_q"], aps["w_k"], aps["w_v"], aps["w_o"]
    alibi_in = aps["alibi_in"]
    if layer == 0:
        lam_in, subw_in = aps["lam_in"], aps["subw_in"]
        KF, VF, EDIM, NUNIT = 1024, 1024, 256, 8
        qcol0, kcol0, vcol0 = 0, 2048, 3072
    else:
        KF, VF, EDIM, NUNIT = 2048, 2048, 128, 16
        qcol0, kcol0, vcol0 = 0, 0, 0

    k_dram = nc.dram_tensor(_uq("k_scr"), [KF, S], BF16, kind="Internal")
    v_dram = nc.dram_tensor(_uq("v_scr"), [S, VF], BF16, kind="Internal")
    o_dram = nc.dram_tensor(_uq("o_scr"), [TOK, D], BF16, kind="Internal")

    TB = 512
    NSUB = TB // 128

    def SB(name, shape, dt):
        return es.enter_context(nc.sbuf_tensor(_uq(name), shape, dt))

    def PS(name, shape, dt=F32):
        return es.enter_context(nc.psum_tensor(_uq(name), shape, dt))

    ident = SB("identb", [128, 128], BF16)
    biasf = SB("biasf", [128, 32], F32)
    qb_sc = SB("qb_sc", [128, 16], F32)
    ones = SB("ones", [1, 128], F32)
    row = SB("row", [1, D], F32)
    bob = SB("bob", [128, D], F32)
    bvb = SB("bvb", [128, VF], F32)
    mst = SB("mst", [128, 16], F32)
    lamn = SB("lamn", [128, 1], F32)
    subwb = SB("subwb", [128, 256], F32)
    obufb = SB("obufb", [128, 256], BF16)
    obuf = SB("obuf", [128, 256], F32)
    qall = SB("qall", [128, 16 * TOK], BF16)

    ps_s0 = PS("ps_s0", [128, KB])
    ps_s1 = PS("ps_s1", [128, KB])
    ps_et0 = PS("ps_et0", [128, KB], BF16)
    ps_et1 = PS("ps_et1", [128, KB], BF16)
    ps_o = PS("ps_o", [128, 1024])
    psS = [ps_s0, ps_s1]
    psE = [ps_et0, ps_et1]
    ps_tr = ps_et0

    p.step("gpsimd", lambda E: E.dma_start(out=ident[:, :], in_=ident_in[:, :]),
           inc="gd", inc_by=16)
    p.step("sync", lambda E: E.dma_start(out=biasf[:, :], in_=b_qkv[:, :]),
           inc="ld", inc_by=16)
    p.step("sync", lambda E: E.dma_start(out=row[:, :], in_=b_out[:, :]),
           inc="ld", inc_by=16)
    p.step("vector", lambda E: E.memset(ones[:, :], 1.0), inc="ve")
    p.wait("vector", "ld")
    p.wait("tensor", "ld")
    p.wait("tensor", "gd")
    p.step("vector", lambda E: E.tensor_scalar_mul(
        qb_sc[:, :], biasf[:, 0:16], DSC), inc="ve")
    p.wait("tensor", "ve")
    emit_row_broadcast(p, ps_o, ones, row, bob, D)
    p.wait("sync", "pe")
    p.step("sync", lambda E: E.dma_start(out=row[0:1, 0:VF],
                                         in_=bv_row[0:1, 0:VF]),
           inc="ld", inc_by=16)
    p.wait("tensor", "ld")
    emit_row_broadcast(p, ps_o, ones, row, bvb, VF)

    if layer == 0:
        lam_sb = SB("lam_sb", [1, 512], F32)
        subr = SB("subr", [1, 256], F32)
        sm = SB("sm", [1, 8], F32)

        p.step("sync", lambda E: E.dma_start(out=lam_sb[:, :], in_=lam_in[:, :]),
               inc="ld", inc_by=16)
        p.step("sync", lambda E: E.dma_start(out=subr[:, :], in_=subw_in[:, :]),
               inc="ld", inc_by=16)
        p.wait("vector", "ld")
        p.step("vector", lambda E: E.tensor_mul(
            lam_sb[0:1, 0:128], lam_sb[0:1, 0:128], lam_sb[0:1, 128:256]),
            inc="ve")
        p.step("vector", lambda E: E.tensor_mul(
            lam_sb[0:1, 256:384], lam_sb[0:1, 256:384], lam_sb[0:1, 384:512]),
            inc="ve")
        p.wait("vector", "ve")
        p.step("vector", lambda E: E.reduce_sum(
            sm[0:1, 0:1], lam_sb[0:1, 0:128], axis=AX.X), inc="ve")
        p.step("vector", lambda E: E.reduce_sum(
            sm[0:1, 1:2], lam_sb[0:1, 256:384], axis=AX.X), inc="ve")
        p.wait("scalar", "ve")
        p.step("scalar", lambda E: E.activation(
            sm[0:1, 2:3], sm[0:1, 0:1], AF.Exp), inc="se")
        p.step("scalar", lambda E: E.activation(
            sm[0:1, 3:4], sm[0:1, 1:2], AF.Exp), inc="se")
        p.wait("vector", "se")
        p.step("vector", lambda E: E.tensor_sub(
            sm[0:1, 4:5], sm[0:1, 3:4], sm[0:1, 2:3]), inc="ve")
        p.wait("vector", "ve")
        p.step("vector", lambda E: E.tensor_scalar_add(
            sm[0:1, 4:5], sm[0:1, 4:5], -LAM_INIT), inc="ve")
        p.wait("tensor", "ve")
        p.step("tensor", lambda E: E.matmul(
            ps_o[:, 0:1], ones[0:1, 0:128], sm[0:1, 4:5],
            start=True, stop=True), inc="pe")
        p.wait("vector", "pe")
        p.step("vector", lambda E: E.tensor_copy(lamn[:, :], ps_o[:, 0:1]),
               inc="ve")
        p.step("vector", lambda E: E.tensor_scalar_mul(
            subr[:, :], subr[:, :], 1.0 - LAM_INIT), inc="ve")
        p.wait("tensor", "ve")
        emit_row_broadcast(p, ps_o, ones, subr, subwb, 256)

    n_kchunk = KF // 128
    pe_use_marks = {}

    with (
        nc.sbuf_tensor(_uq("x_tm"), [128, NSUB * D], F32) as x_tm,
        nc.sbuf_tensor(_uq("x_tmb"), [128, NSUB * D], BF16) as x_tmb,
        nc.sbuf_tensor(_uq("x_fm"), [128, 16 * TB], BF16) as x_fm,
        nc.sbuf_tensor(_uq("wbuf"), [128, 2 * 16 * 128], BF16) as wbuf,
        nc.sbuf_tensor(_uq("wvbuf"), [128, 2 * 16 * 512], BF16) as wvbuf,
        nc.sbuf_tensor(_uq("evac"), [128, 2 * 512], BF16) as evac,
    ):
        ev_ctr = [0]
        ev_marks = {}

        def ev_half():
            i = ev_ctr[0]
            ev_ctr[0] += 1
            return i, (i % 2) * 512

        def load_w_lhsT(half, w_ap, fbase):
            p.step("gpsimd", lambda E, half=half, w_ap=w_ap, fbase=fbase:
                   E.dma_start(
                       out=wbuf[:, half * 2048:(half + 1) * 2048],
                       in_=bass.AP(w_ap.tensor, fbase,
                                   [[w_ap.shape[1], 128],
                                    [128 * w_ap.shape[1], 16],
                                    [1, 128]])), inc="gd", inc_by=16)

        def load_w_rhs(half, w_ap, fbase, fw=512):
            p.step("gpsimd", lambda E, half=half, w_ap=w_ap, fbase=fbase, fw=fw:
                   E.dma_start(
                       out=wvbuf[:, half * 8192: half * 8192 + 16 * fw],
                       in_=bass.AP(w_ap.tensor, fbase,
                                   [[w_ap.shape[1], 128],
                                    [128 * w_ap.shape[1], 16],
                                    [1, fw]])), inc="gd", inc_by=16)

        def fm_matmul_into(ps, wbh, last_inc=True):
            for c in range(16):
                p.step("tensor", lambda E, c=c, ps=ps, wbh=wbh: E.matmul(
                    ps[:, 0:TB],
                    wbuf[:, wbh * 2048 + c * 128: wbh * 2048 + (c + 1) * 128],
                    x_fm[:, c * TB:(c + 1) * TB],
                    start=(c == 0), stop=(c == 15)),
                    inc="pe" if (c == 15 and last_inc) else None)

        for tb in range(S // TB + 4):
            is_q = tb >= S // TB
            qb = tb - S // TB
            p.wait("sync", "ve")
            p.wait("sync", "pe")
            if not is_q:
                p.step("sync", lambda E, tb=tb: E.dma_start(
                    out=x_tm[:, :],
                    in_=bass.AP(xb.tensor, tb * TB * D,
                                [[D, 128], [128 * D, NSUB], [1, D]])),
                    inc="ld", inc_by=16)
            else:
                p.step("sync", lambda E, qb=qb: E.dma_start(
                    out=x_tm[:, :],
                    in_=bass.AP(xq.tensor, qb * TB * D,
                                [[D, 128], [128 * D, NSUB], [1, D]])),
                    inc="ld", inc_by=16)
            p.wait("vector", "ld")
            p.wait("vector", "pe")
            for st in range(NSUB):
                p.step("vector", lambda E, st=st: E.tensor_copy(
                    x_tmb[:, st * D:(st + 1) * D], x_tm[:, st * D:(st + 1) * D]),
                    inc="ve")
            p.wait("tensor", "ve")
            emit_tm2fm(p, x_tmb, x_fm, ident, NSUB, ps_tr[:, 0:512], tcols=TB)
            xfm_ok = p.cnt["ve"]

            if not is_q:
                for kc in range(n_kchunk):
                    half = kc % 2
                    p.wait("gpsimd", "pe")
                    load_w_lhsT(half, w_k, kcol0 + kc * 128)
                    p.wait("tensor", "gd")
                    p.wait("tensor", "ve", xfm_ok)
                    p.wait("tensor", "se")
                    fm_matmul_into(ps_s0, half)
                    pe_use_marks[("k", half)] = p.cnt["pe"]
                    p.wait("scalar", "pe")
                    p.step("scalar", lambda E, kc=kc: E.activation(
                        evac[:, 0:TB], ps_s0[:, 0:TB], AF.Identity,
                        bias=biasf[:, 16 + kc:17 + kc]), inc="se")
                    p.wait("sync", "se")
                    p.step("sync", lambda E, kc=kc, tb=tb: E.dma_start(
                        out=bass.AP(k_dram, kc * 128 * S + tb * TB,
                                    [[S, 128], [1, TB]]),
                        in_=evac[:, 0:TB]), inc="st", inc_by=16)
                for vg in range(VF // 512):
                    half = vg % 2
                    p.wait("gpsimd", "pe")
                    load_w_rhs(half, w_v, vcol0 + vg * 512)
                    p.wait("tensor", "gd")
                    p.wait("tensor", "ve", xfm_ok)
                    for st in range(NSUB):
                        p.wait("tensor", "ve")
                        for c in range(16):
                            p.step("tensor",
                                   lambda E, c=c, st=st, half=half: E.matmul(
                                       ps_s1[:, 0:512],
                                       x_fm[:, c * TB + st * 128:
                                            c * TB + (st + 1) * 128],
                                       wvbuf[:, half * 8192 + c * 512:
                                             half * 8192 + (c + 1) * 512],
                                       start=(c == 0), stop=(c == 15)),
                                   inc="pe" if c == 15 else None)
                        if st == NSUB - 1:
                            pe_use_marks[("v", half)] = p.cnt["pe"]
                        p.wait("vector", "pe")
                        ei, eo = ev_half()
                        p.wait("vector", "st", ev_marks.get(ei - 2, 0))
                        p.step("vector", lambda E, vg=vg, eo=eo: E.tensor_add(
                            evac[:, eo:eo + 512], ps_s1[:, 0:512],
                            bvb[:, vg * 512:(vg + 1) * 512]), inc="ve")
                        p.wait("sync", "ve")
                        ev_marks[ei] = p.step("sync",
                            lambda E, vg=vg, tb=tb, st=st, eo=eo: E.dma_start(
                                out=bass.AP(v_dram,
                                            (tb * TB + st * 128) * VF + vg * 512,
                                            [[VF, 128], [1, 512]]),
                                in_=evac[:, eo:eo + 512]), inc="st", inc_by=16)
            else:
                for qc in range(16):
                    half = qc % 2
                    p.wait("gpsimd", "pe")
                    load_w_lhsT(half, w_q, qcol0 + qc * 128)
                    p.wait("tensor", "gd")
                    p.wait("tensor", "ve", xfm_ok)
                    p.wait("tensor", "se")
                    fm_matmul_into(ps_s0, half)
                    pe_use_marks[("k", half)] = p.cnt["pe"]
                    p.wait("scalar", "pe")
                    p.step("scalar", lambda E, qc=qc, qb=qb: E.activation(
                        qall[:, qc * TOK + qb * TB: qc * TOK + (qb + 1) * TB],
                        ps_s0[:, 0:TB], AF.Identity, scale=DSC,
                        bias=qb_sc[:, qc:qc + 1]), inc="se")
        p.barrier()

    if dev:
        vmdump = nc.dram_tensor("vmdump", [128, (S // 128) * (EDIM + 1)], BF16,
                                kind="ExternalOutput").ap()
        odump_raw = nc.dram_tensor("odump_raw", [16, NT, 128, EDIM + 1], F32,
                                   kind="ExternalOutput").ap()
    vmap_holder = []
    with (
        nc.sbuf_tensor(_uq("kmap"), [128, 2 * S if layer == 0 else S], BF16) as kmap,
        nc.sbuf_tensor(_uq("vmap"), [128, (S // 128) * (EDIM + 1)], BF16) as vmap,
        nc.sbuf_tensor(_uq("masks_sb"), [128, 2 * KB], F32) as masks,
        nc.sbuf_tensor(_uq("Esb"), [128, 2 * KB], BF16) as Esb,
        nc.sbuf_tensor(_uq("etsb"), [128, 2 * KB], BF16) as etsb,
        nc.sbuf_tensor(_uq("aseg"), [1, S], F16) as aseg,
        nc.sbuf_tensor(_uq("ones16"), [1, 128], F16) as ones16,
    ):
        p.step("vector", lambda E: E.memset(ones16[:, :], 1.0), inc="ve")
        p.step("sync", lambda E: E.dma_start(
            out=masks[:, :],
            in_=bass.AP(masks_in.tensor, 0,
                        [[KB, 128], [128 * KB, 2], [1, KB]])),
            inc="ld", inc_by=16)
        p.wait("vector", "ld")

        vmap_holder.append(vmap)
        ncomp = 2 if layer == 0 else 1
        state = {"t": 0, "se_exp": {}, "se_evac": {}, "pe_tr": {},
                 "pe_av": 0}

        def flash_tile(mapi, comp, j, qcb, kmap_off, ocol, state):
            nb = j // 2 + 1
            for kb in range(nb):
                t = state["t"]
                si = t % 2
                Sps = psS[si]
                Eps = psE[si]
                diag = (kb == nb - 1)
                C = _alibi_shift(mapi, kb) if layer == 1 else 0.0
                mold, mnew = (kb + 1) % 2, kb % 2
                p.wait("tensor", "se", state["se_exp"].get(t - 2, 0))
                for hh in range(2):
                    last = (hh == 1 and layer == 0)
                    p.step("tensor",
                           lambda E, j=j, kb=kb, hh=hh, Sps=Sps, qcb=qcb,
                           kmap_off=kmap_off: E.matmul(
                               Sps[:, hh * 512:(hh + 1) * 512],
                               qall[:, qcb * TOK + j * 128:
                                    qcb * TOK + (j + 1) * 128],
                               kmap[:, kmap_off + kb * KB + hh * 512:
                                    kmap_off + kb * KB + (hh + 1) * 512],
                               start=True, stop=(layer == 0)),
                           inc="pe" if last else None)
                    if layer == 1:
                        p.step("tensor",
                               lambda E, kb=kb, hh=hh, Sps=Sps: E.matmul(
                                   Sps[:, hh * 512:(hh + 1) * 512],
                                   ones16[0:1, :],
                                   aseg[0:1, kb * KB + hh * 512:
                                        kb * KB + (hh + 1) * 512],
                                   start=False, stop=True),
                               inc="pe" if hh == 1 else None)
                pe_mm = p.cnt["pe"]

                p.wait("vector", "pe", pe_mm)
                if diag:
                    par = j % 2
                    p.step("vector", lambda E, Sps=Sps, par=par: E.tensor_add(
                        Sps[:, :], Sps[:, :],
                        masks[:, par * KB:(par + 1) * KB]), inc="ve")
                    p.wait("vector", "ve")
                p.step("vector", lambda E, Sps=Sps: E.reduce_max(
                    mst[:, 2:3], Sps[:, :], axis=AX.X, negate=True), inc="ve")
                p.wait("vector", "ve")
                if kb == 0:
                    if C != 0.0:
                        p.step("vector", lambda E, C=C, mnew=mnew:
                               E.tensor_scalar_add(
                                   mst[:, mnew:mnew + 1], mst[:, 2:3], -C),
                               inc="ve")
                    else:
                        p.step("vector", lambda E, mnew=mnew: E.tensor_copy(
                            mst[:, mnew:mnew + 1], mst[:, 2:3]), inc="ve")
                else:
                    p.step("vector",
                           lambda E, C=C, mold=mold, mnew=mnew:
                           E.scalar_tensor_tensor(
                               mst[:, mnew:mnew + 1], mst[:, 2:3], -C,
                               mst[:, mold:mold + 1], ALU.add, ALU.min),
                           inc="ve")
                if C != 0.0:
                    p.wait("vector", "ve")
                    p.step("vector", lambda E, C=C, mnew=mnew: E.tensor_scalar_add(
                        mst[:, 3:4], mst[:, mnew:mnew + 1], C), inc="ve")
                ebc = 3 if C != 0.0 else mnew
                ve_state = p.cnt["ve"]

                p.wait("scalar", "ve", ve_state)
                if kb > 0:
                    p.step("scalar",
                           lambda E, mold=mold, mnew=mnew: E.activation(
                               mst[:, 4:5], mst[:, mold:mold + 1], AF.Exp,
                               scale=-1.0, bias=mst[:, mnew:mnew + 1]),
                           inc="se")
                p.wait("scalar", "pe", state["pe_tr"].get(t - 2, 0))
                p.step("scalar", lambda E, Sps=Sps, si=si, ebc=ebc: E.activation(
                    Esb[:, si * KB:(si + 1) * KB], Sps[:, :], AF.Exp,
                    bias=mst[:, ebc:ebc + 1]), inc="se")
                se_exp = p.cnt["se"]
                state["se_exp"][t] = se_exp

                p.wait("vector", "se", se_exp)
                if kb > 0:
                    p.wait("vector", "pe", state["pe_av"])
                    p.step("vector", lambda E, ocol=ocol: E.tensor_scalar_mul(
                        ps_o[:, ocol:ocol + EDIM + 1],
                        ps_o[:, ocol:ocol + EDIM + 1],
                        mst[:, 4:5]), inc="ve")
                ve_resc = p.cnt["ve"]

                p.wait("tensor", "se", se_exp)
                p.wait("tensor", "se", state["se_evac"].get(t - 2, 0))
                for u in range(8):
                    p.step("tensor", lambda E, u=u, si=si, Eps=Eps: E.transpose(
                        Eps[:, u * 128:(u + 1) * 128],
                        Esb[:, si * KB + u * 128: si * KB + (u + 1) * 128],
                        ident[:, :]), inc="pe" if u == 7 else None)
                pe_tr = p.cnt["pe"]
                state["pe_tr"][t] = pe_tr

                p.wait("scalar", "pe", pe_tr)
                p.wait("scalar", "pe", state["pe_av"])
                p.step("scalar", lambda E, si=si, Eps=Eps: E.copy(
                    etsb[:, si * KB:(si + 1) * KB], Eps[:, :]), inc="se")
                se_evac = p.cnt["se"]
                state["se_evac"][t] = se_evac

                p.wait("tensor", "se", se_evac)
                p.wait("tensor", "ve", ve_resc)
                for u in range(8):
                    p.step("tensor",
                           lambda E, u=u, kb=kb, si=si, ocol=ocol: E.matmul(
                               ps_o[:, ocol:ocol + EDIM + 1],
                               etsb[:, si * KB + u * 128:
                                    si * KB + (u + 1) * 128],
                               vmap[:, (kb * 8 + u) * (EDIM + 1):
                                    (kb * 8 + u + 1) * (EDIM + 1)],
                               start=(kb == 0 and u == 0),
                               stop=(kb == nb - 1 and u == 7)),
                           inc="pe" if u == 7 else None)
                state["pe_av"] = p.cnt["pe"]
                state["t"] = t + 1

        nmaps = NUNIT
        for mi in range(nmaps):
            if layer == 0:
                kvg = mi // 2
                kf0 = kvg * 256
                vf0 = kvg * 256
                need_kv = (mi % 2 == 0)
            else:
                kf0 = mi * 128
                vf0 = mi * 128
                need_kv = True
            if need_kv:
                p.wait("sync", "pe")
                nk = 2 if layer == 0 else 1
                for cc in range(nk):
                    p.step("sync", lambda E, kf0=kf0, cc=cc: E.dma_start(
                        out=kmap[:, cc * S:(cc + 1) * S],
                        in_=bass.AP(k_dram, (kf0 + cc * 128) * S,
                                    [[S, 128], [1, S]])), inc="ld", inc_by=16)
                NSPL = 8
                csz = (S // 128) // NSPL
                for sp in range(NSPL):
                    p.step("sync", lambda E, vf0=vf0, sp=sp, csz=csz: E.dma_start(
                        out=bass.AP(vmap, sp * csz * (EDIM + 1),
                                    [[vmap.shape[1], 128],
                                     [EDIM + 1, csz], [1, EDIM]]),
                        in_=bass.AP(v_dram, vf0 + sp * csz * 128 * VF,
                                    [[VF, 128], [128 * VF, csz], [1, EDIM]])),
                        inc="ld", inc_by=16)
                kv_ok = p.cnt["ld"]
                p.wait("tensor", "ld", kv_ok)
                p.wait("vector", "ld")
                p.wait("vector", "pe")
                if layer == 1:
                    p.step("sync", lambda E, mi=mi: E.dma_start(
                        out=aseg[:, :], in_=alibi_in[mi:mi + 1, :]),
                        inc="ld", inc_by=16)
                    p.wait("tensor", "ld")
                p.step("vector", lambda E: E.memset(
                    bass.AP(vmap, EDIM, [[vmap.shape[1], 128],
                                         [EDIM + 1, S // 128]]), 1.0),
                    inc="ve")
                p.wait("tensor", "ve")
                if dev and layer == 1 and mi == NUNIT - 1:
                    p.wait("gpsimd", "ve")
                    p.step("gpsimd", lambda E: E.dma_start(
                        out=vmdump[:, :], in_=vmap[:, :]), inc="gd", inc_by=16)
                    p.wait("vector", "gd")

            if layer == 0:
                pass

            for j in range(NT):
                if layer == 0:
                    unit = mi
                    flash_tile(unit, 0, j, unit * 2 + 0, 0, 0, state)
                    p.wait("vector", "pe", state["pe_av"])
                    p.wait("vector", "st")
                    p.step("vector", lambda E: E.reciprocal(
                        mst[:, 8:9], ps_o[:, EDIM:EDIM + 1]), inc="ve")
                    p.wait("vector", "ve")
                    p.step("vector", lambda E: E.tensor_scalar_mul(
                        obuf[:, 0:EDIM], ps_o[:, 0:EDIM], mst[:, 8:9]),
                        inc="ve")
                    p.wait("tensor", "ve")
                    flash_tile(unit, 1, j, unit * 2 + 1, S, 512, state)
                    p.wait("vector", "pe", state["pe_av"])
                    p.step("vector", lambda E: E.reciprocal(
                        mst[:, 9:10], ps_o[:, 512 + EDIM:513 + EDIM]),
                        inc="ve")
                    p.wait("vector", "ve")
                    p.step("vector", lambda E: E.tensor_mul(
                        mst[:, 10:11], lamn[:, :], mst[:, 9:10]), inc="ve")
                    p.wait("vector", "ve")
                    p.step("vector", lambda E: E.scalar_tensor_tensor(
                        obuf[:, 0:EDIM], ps_o[:, 512:512 + EDIM],
                        mst[:, 10:11], obuf[:, 0:EDIM],
                        ALU.mult, ALU.add), inc="ve")
                    ve_m = p.cnt["ve"]
                    p.wait("scalar", "ve", ve_m)
                    p.step("scalar", lambda E: E.activation(
                        obufb[:, 0:EDIM], obuf[:, 0:EDIM], AF.Square,
                        accum_out=mst[:, 11:12]), inc="se")
                    p.wait("vector", "se")
                    p.step("vector", lambda E: E.tensor_scalar(
                        mst[:, 11:12], mst[:, 11:12], 1.0 / EDIM, 1e-5,
                        ALU.mult, ALU.add), inc="ve")
                    p.wait("scalar", "ve")
                    p.step("scalar", lambda E: E.activation(
                        mst[:, 12:13], mst[:, 11:12], AF.Sqrt), inc="se")
                    p.wait("vector", "se")
                    p.step("vector", lambda E: E.reciprocal(
                        mst[:, 13:14], mst[:, 12:13]), inc="ve")
                    p.wait("vector", "ve")
                    p.step("vector", lambda E: E.scalar_tensor_tensor(
                        obufb[:, 0:EDIM], obuf[:, 0:EDIM], mst[:, 13:14],
                        subwb[:, :], ALU.mult, ALU.mult), inc="ve")
                    ve_o = p.cnt["ve"]
                    p.wait("sync", "ve", ve_o)
                    p.step("sync", lambda E, unit=unit, j=j: E.dma_start(
                        out=bass.AP(o_dram, j * 128 * D + unit * 256,
                                    [[D, 128], [1, 256]]),
                        in_=obufb[:, 0:EDIM]), inc="st", inc_by=16)
                    p.wait("tensor", "st")
                else:
                    flash_tile(mi, 0, j, mi, 0, 0, state)
                    p.wait("vector", "pe", state["pe_av"])
                    p.wait("vector", "st")
                    if dev:
                        p.step("vector", lambda E: E.tensor_copy(
                            obuf[:, 0:EDIM + 1], ps_o[:, 0:EDIM + 1]), inc="ve")
                        p.wait("sync", "ve")
                        p.step("sync", lambda E, mi=mi, j=j: E.dma_start(
                            out=odump_raw[mi, j, :, :], in_=obuf[:, 0:EDIM + 1]),
                            inc="st", inc_by=16)
                        p.wait("vector", "st")
                    p.step("vector", lambda E: E.reciprocal(
                        mst[:, 7:8], ps_o[:, EDIM:EDIM + 1]), inc="ve")
                    p.wait("vector", "ve")
                    p.step("vector", lambda E: E.tensor_scalar_mul(
                        obufb[:, 0:EDIM], ps_o[:, 0:EDIM], mst[:, 7:8]),
                        inc="ve")
                    ve_o = p.cnt["ve"]
                    p.wait("sync", "ve", ve_o)
                    p.step("sync", lambda E, mi=mi, j=j: E.dma_start(
                        out=bass.AP(o_dram, j * 128 * D + mi * 128,
                                    [[D, 128], [1, 128]]),
                        in_=obufb[:, 0:EDIM]), inc="st", inc_by=16)
                    p.wait("tensor", "st")
        p.barrier()

    with (
        nc.sbuf_tensor(_uq("wo_sb"), [128, 16 * 2048], BF16) as wo_sb,
        nc.sbuf_tensor(_uq("o_tile"), [128, D], BF16) as o_tile,
        nc.sbuf_tensor(_uq("o_fm"), [128, 16 * 128], BF16) as o_fm,
        nc.sbuf_tensor(_uq("xres"), [128, D], F32) as xres,
        nc.sbuf_tensor(_uq("ysb"), [128, D], F32) as ysb,
        nc.sbuf_tensor(_uq("sq"), [128, D], BF16) as sq,
        nc.sbuf_tensor(_uq("xo_sb"), [128, D], F32) as xo_sb,
    ):
        p.step("gpsimd", lambda E: E.dma_start(
            out=wo_sb[:, :],
            in_=bass.AP(w_o.tensor, 0,
                        [[D, 128], [128 * D, 16], [1, D]])), inc="gd", inc_by=16)
        p.wait("tensor", "gd")
        for j in range(NT):
            p.wait("sync", "pe")
            p.wait("sync", "ve")
            p.wait("sync", "se")
            p.step("sync", lambda E, j=j: E.dma_start(
                out=o_tile[:, :],
                in_=bass.AP(o_dram, j * 128 * D, [[D, 128], [1, D]])),
                inc="ld", inc_by=16)
            p.step("sync", lambda E, j=j: E.dma_start(
                out=xres[:, :],
                in_=bass.AP(xq.tensor, j * 128 * D,
                            [[D, 128], [1, D]])), inc="ld", inc_by=16)
            p.wait("tensor", "ld")
            p.wait("vector", "ld")
            p.wait("vector", "se")
            for dg in range(4):
                p.step("vector", lambda E, dg=dg: E.tensor_add(
                    xres[:, dg * 512:(dg + 1) * 512],
                    xres[:, dg * 512:(dg + 1) * 512],
                    bob[:, dg * 512:(dg + 1) * 512]), inc="ve")
            p.wait("vector", "ve")
            emit_tm2fm(p, o_tile, o_fm, ident, 1, ps_tr[:, 0:512], tcols=128,
                       src_stride=D)
            for dg in range(4):
                p.wait("tensor", "ve")
                for ec in range(16):
                    p.step("tensor", lambda E, ec=ec, dg=dg: E.matmul(
                        ps_s0[:, 0:512],
                        o_fm[:, ec * 128:(ec + 1) * 128],
                        wo_sb[:, ec * 2048 + dg * 512: ec * 2048 + (dg + 1) * 512],
                        start=(ec == 0), stop=(ec == 15)),
                        inc="pe" if ec == 15 else None)
                p.wait("vector", "pe")
                p.step("vector", lambda E, dg=dg: E.tensor_add(
                    ysb[:, dg * 512:(dg + 1) * 512], ps_s0[:, 0:512],
                    xres[:, dg * 512:(dg + 1) * 512]), inc="ve")
            p.wait("scalar", "st")
            emit_ln_epilogue(p, ysb, sq, xo_sb, mst)
            p.wait("sync", "se")
            p.step("sync", lambda E, j=j: E.dma_start(
                out=bass.AP(xout.tensor, j * 128 * D, [[D, 128], [1, D]]),
                in_=xo_sb[:, :]), inc="st", inc_by=16)
    if dev:
        kdump = nc.dram_tensor("kdump", [KF, S], BF16, kind="ExternalOutput").ap()
        vdump = nc.dram_tensor("vdump", [S, VF], BF16, kind="ExternalOutput").ap()
        odump = nc.dram_tensor("odump", [TOK, D], BF16, kind="ExternalOutput").ap()
        qdump = nc.dram_tensor("qdump", [128, 16 * TOK], BF16,
                               kind="ExternalOutput").ap()
        p.barrier(("gpsimd",))
        p.step("gpsimd", lambda E: E.dma_start(out=kdump[:, :], in_=k_dram.ap()),
               inc="gd", inc_by=16)
        p.step("gpsimd", lambda E: E.dma_start(out=vdump[:, :], in_=v_dram.ap()),
               inc="gd", inc_by=16)
        p.step("gpsimd", lambda E: E.dma_start(out=odump[:, :], in_=o_dram.ap()),
               inc="gd", inc_by=16)
        p.step("gpsimd", lambda E: E.dma_start(out=qdump[:, :], in_=qall[:, :]),
               inc="gd", inc_by=16)
        p.wait("gpsimd", "gd")
    es.close()



def emit_swiglu_block(p, nc, bufs, x_fm, macc, w_up1, w_up3, w_down, nh,
                      up_off1=0, up_off3=0, up_base=0, down_base=0,
                      up_stride=None, down_stride=D, acc_scale=None):
    wbuf, hsil, hsup, wdg = bufs["wbuf"], bufs["hsil"], bufs["hsup"], bufs["wdg"]
    ps_a, ps_b, ps_c = bufs["ps_a"], bufs["ps_b"], bufs["ps_c"]
    if up_stride is None:
        up_stride = w_up1.shape[-1]
    TBF = 512
    NH_CH = nh // 128
    st_marks = bufs.setdefault("marks", {})
    for hs in range((NH_CH + 15) // 16):
        nch = min(16, NH_CH - hs * 16)
        for hc in range(nch):
            h = hs * 16 + hc
            half = h % 2
            p.wait("gpsimd", "pe")
            for q4 in range(4):
                p.step("gpsimd", lambda E, h=h, half=half, w=w_up1,
                       o=up_base + up_off1, us=up_stride, q4=q4:
                       E.dma_start(
                           out=wbuf[:, half * 4096 + q4 * 512:
                                half * 4096 + (q4 + 1) * 512],
                           in_=bass.AP(w.tensor, o + h * 128 + q4 * 4 * 128 * us,
                                       [[us, 128], [128 * us, 4], [1, 128]])),
                       inc="gd", inc_by=16)
                p.step("gpsimd", lambda E, h=h, half=half, w=w_up3,
                       o=up_base + up_off3, us=up_stride, q4=q4:
                       E.dma_start(
                           out=wbuf[:, half * 4096 + 2048 + q4 * 512:
                                half * 4096 + 2048 + (q4 + 1) * 512],
                           in_=bass.AP(w.tensor, o + h * 128 + q4 * 4 * 128 * us,
                                       [[us, 128], [128 * us, 4], [1, 128]])),
                       inc="gd", inc_by=16)
            p.wait("tensor", "gd")
            p.wait("tensor", "se")
            p.wait("tensor", "ve")
            for c in range(16):
                p.step("tensor", lambda E, c=c, half=half: E.matmul(
                    ps_a[:, 0:TBF],
                    wbuf[:, half * 4096 + c * 128:half * 4096 + (c + 1) * 128],
                    x_fm[:, c * TBF:(c + 1) * TBF],
                    start=(c == 0), stop=(c == 15)), inc=None)
            for c in range(16):
                p.step("tensor", lambda E, c=c, half=half: E.matmul(
                    ps_b[:, 0:TBF],
                    wbuf[:, half * 4096 + 2048 + c * 128:
                         half * 4096 + 2048 + (c + 1) * 128],
                    x_fm[:, c * TBF:(c + 1) * TBF],
                    start=(c == 0), stop=(c == 15)),
                    inc="pe" if c == 15 else None)
            st_marks[("u", half)] = p.cnt["pe"]
            p.wait("scalar", "pe")
            p.wait("scalar", "ve")
            p.step("scalar", lambda E: E.activation(
                hsil[:, 0:TBF], ps_a[:, 0:TBF], AF.Silu), inc="se")
            p.wait("vector", "se")
            p.wait("vector", "pe")
            p.step("vector", lambda E, hc=hc: E.tensor_mul(
                hsup[:, hc * TBF:(hc + 1) * TBF], hsil[:, 0:TBF],
                ps_b[:, 0:TBF]), inc="ve")
        hs_ready = p.cnt["ve"]
        for dg in range(4):
            halfd = dg % 2
            p.wait("gpsimd", "pe")
            nspl = (nch + 3) // 4
            for q4 in range(nspl):
                nc4 = min(4, nch - q4 * 4)
                p.step("gpsimd", lambda E, hs=hs, dg=dg, halfd=halfd, w=w_down,
                       nc4=nc4, q4=q4, db=down_base, ds=down_stride:
                       E.dma_start(
                           out=wdg[:, halfd * 8192 + (q4 * 4) * 512:
                                halfd * 8192 + (q4 * 4 + nc4) * 512],
                           in_=bass.AP(w.tensor,
                                       db + (hs * 16 + q4 * 4) * 128 * ds + dg * 512,
                                       [[ds, 128], [128 * ds, nc4], [1, 512]])),
                       inc="gd", inc_by=16)
            p.wait("tensor", "gd")
            p.wait("tensor", "ve", hs_ready)
            for tsub in range(4):
                p.wait("tensor", "ve")
                for hc in range(nch):
                    p.step("tensor",
                           lambda E, hc=hc, tsub=tsub, halfd=halfd, nch=nch:
                           E.matmul(
                               ps_c[:, 0:512],
                               hsup[:, hc * TBF + tsub * 128:
                                    hc * TBF + (tsub + 1) * 128],
                               wdg[:, halfd * 8192 + hc * 512:
                                    halfd * 8192 + (hc + 1) * 512],
                               start=(hc == 0), stop=(hc == nch - 1)),
                           inc="pe" if hc == nch - 1 else None)
                if tsub == 3:
                    st_marks[("d", halfd)] = p.cnt["pe"]
                p.wait("vector", "pe")
                if acc_scale is None:
                    p.step("vector", lambda E, tsub=tsub, dg=dg: E.tensor_add(
                        macc[:, tsub * D + dg * 512:tsub * D + (dg + 1) * 512],
                        macc[:, tsub * D + dg * 512:tsub * D + (dg + 1) * 512],
                        ps_c[:, 0:512]), inc="ve")
                else:
                    p.step("vector",
                           lambda E, tsub=tsub, dg=dg, sc=acc_scale(tsub):
                           E.scalar_tensor_tensor(
                               macc[:, tsub * D + dg * 512:
                                    tsub * D + (dg + 1) * 512],
                               ps_c[:, 0:512], sc,
                               macc[:, tsub * D + dg * 512:
                                    tsub * D + (dg + 1) * 512],
                               ALU.mult, ALU.add), inc="ve")


def emit_swiglu_driver(p, nc, xq, xout, ident, mst, psl, per_tb_fn,
                       macc_dump=None):
    ps_a, ps_b, ps_c, ps_tr = psl
    NSUB = 4
    TBF = 512
    with (
        nc.sbuf_tensor(_uq("f_x_tm"), [128, NSUB * D], F32) as x_tm,
        nc.sbuf_tensor(_uq("f_x_tmb"), [128, NSUB * D], BF16) as x_tmb,
        nc.sbuf_tensor(_uq("f_x_fm"), [128, 16 * TBF], BF16) as x_fm,
        nc.sbuf_tensor(_uq("f_wbuf"), [128, 2 * 2 * 2048], BF16) as wbuf,
        nc.sbuf_tensor(_uq("f_hsil"), [128, TBF], F32) as hsil,
        nc.sbuf_tensor(_uq("f_hsup"), [128, 16 * TBF], BF16) as hsup,
        nc.sbuf_tensor(_uq("f_wdg"), [128, 2 * 16 * 512], BF16) as wdg,
        nc.sbuf_tensor(_uq("f_macc"), [128, NSUB * D], F32) as macc,
        nc.sbuf_tensor(_uq("f_ysb"), [128, D], F32) as ysb,
        nc.sbuf_tensor(_uq("f_sq"), [128, D], BF16) as sq,
        nc.sbuf_tensor(_uq("f_xo"), [128, D], F32) as xo_sb,
    ):
        bufs = {"wbuf": wbuf, "hsil": hsil, "hsup": hsup, "wdg": wdg,
                "ps_a": ps_a, "ps_b": ps_b, "ps_c": ps_c, "x_tm": x_tm,
                "x_tmb": x_tmb}
        for tb in range(4):
            p.wait("sync", "ve")
            p.wait("sync", "pe")
            p.wait("sync", "se")
            p.step("sync", lambda E, tb=tb: E.dma_start(
                out=x_tm[:, :],
                in_=bass.AP(xq.tensor, tb * TBF * D,
                            [[D, 128], [128 * D, NSUB], [1, D]])),
                inc="ld", inc_by=16)
            p.wait("vector", "ld")
            p.wait("vector", "pe")
            for st in range(NSUB):
                p.step("vector", lambda E, st=st: E.tensor_copy(
                    x_tmb[:, st * D:(st + 1) * D],
                    x_tm[:, st * D:(st + 1) * D]), inc="ve")
            p.wait("tensor", "ve")
            emit_tm2fm(p, x_tmb, x_fm, ident, NSUB, ps_tr[:, 0:512], tcols=TBF)
            p.wait("vector", "se")
            p.step("vector", lambda E: E.memset(macc[:, :], 0.0), inc="ve")
            p.wait("tensor", "ve")
            p.barrier()
            per_tb_fn(tb, x_fm, macc, bufs)
            p.barrier()
            if macc_dump is not None:
                p.step("sync", lambda E, tb=tb: E.dma_start(
                    out=macc_dump[tb, :, :], in_=macc[:, :]),
                    inc="st", inc_by=16)
                p.wait("vector", "st")
                p.wait("scalar", "st")
            for tsub in range(NSUB):
                p.wait("vector", "pe")
                p.wait("vector", "se")
                for dgg in range(4):
                    p.step("vector", lambda E, tsub=tsub, dgg=dgg: E.tensor_add(
                        ysb[:, dgg * 512:(dgg + 1) * 512],
                        macc[:, tsub * D + dgg * 512:tsub * D + (dgg + 1) * 512],
                        x_tm[:, tsub * D + dgg * 512:tsub * D + (dgg + 1) * 512]),
                        inc="ve")
                p.wait("scalar", "st")
                p.barrier(("vector", "scalar"))
                emit_ln_epilogue(p, ysb, sq, xo_sb, mst)
                p.wait("sync", "se")
                p.step("sync", lambda E, tb=tb, tsub=tsub: E.dma_start(
                    out=bass.AP(xout.tensor, (tb * NSUB + tsub) * 128 * D,
                                [[D, 128], [1, D]]),
                    in_=xo_sb[:, :]), inc="st", inc_by=16)
                p.barrier(("vector", "scalar"))


def declare_ffn_inputs(nc, shared):
    a = {}
    a["xq"] = shared.get("xq") or nc.dram_tensor(
        "xq", [TOK, D], F32, kind="ExternalInput").ap()
    a["ident_in"] = shared.get("ident") or nc.dram_tensor(
        "ident", [128, 128], F32, kind="ExternalInput").ap()
    a["w1t"] = nc.dram_tensor("w1t", [D, 8192], F32, kind="ExternalInput").ap()
    a["w3t"] = nc.dram_tensor("w3t", [D, 8192], F32, kind="ExternalInput").ap()
    a["w2t"] = nc.dram_tensor("w2t", [8192, D], F32, kind="ExternalInput").ap()
    a["h128"] = nc.dram_tensor("h128", [128, 128], F32, kind="ExternalInput").ap()
    a["h64"] = nc.dram_tensor("h64", [64, 64], F32, kind="ExternalInput").ap()
    return a


def build_ffn_nc(dev=False):
    nc = bass.Bass("TRN2", target_bir_lowering=False, debug=False)
    p = Prog(nc)
    aps = declare_ffn_inputs(nc, {})
    xout = nc.dram_tensor("xout", [TOK, D], F32, kind="ExternalOutput").ap()
    emit_ffn(p, nc, aps, xout, dev=dev)
    p.wait("gpsimd", "st")
    return p.finalize()


def emit_ffn(p, nc, aps, xout, dev=False):
    import contextlib
    es = contextlib.ExitStack()
    xq, ident_in = aps["xq"], aps["ident_in"]
    w1t, w3t, w2t = aps["w1t"], aps["w3t"], aps["w2t"]
    h128_in, h64_in = aps["h128"], aps["h64"]

    t1_dram = nc.dram_tensor(_uq("t1_scr"), [128, 64 * D], BF16, kind="Internal")
    w2h_dram = nc.dram_tensor(_uq("w2h_scr"), [8192, D], BF16, kind="Internal")

    def SB(name, shape, dt):
        return es.enter_context(nc.sbuf_tensor(_uq(name), shape, dt))

    def PS(name, shape, dt=F32):
        return es.enter_context(nc.psum_tensor(_uq(name), shape, dt))

    ident = SB("identb", [128, 128], BF16)
    h128b = SB("h128b", [128, 128], BF16)
    h64b = SB("h64b", [64, 64], BF16)
    mst = SB("mst", [128, 16], F32)

    ps_a = PS("ps_a", [128, 512])
    ps_b = PS("ps_b", [128, 512])
    ps_c = PS("ps_c", [128, 512])
    ps_tr = PS("ps_tr", [128, 512], BF16)

    p.step("gpsimd", lambda E: E.dma_start(out=ident[:, :], in_=ident_in[:, :]),
           inc="gd", inc_by=16)
    p.step("gpsimd", lambda E: E.dma_start(out=h128b[:, :], in_=h128_in[:, :]),
           inc="gd", inc_by=16)
    p.step("gpsimd", lambda E: E.dma_start(out=h64b[:, :], in_=h64_in[:, :]),
           inc="gd", inc_by=16)
    p.wait("tensor", "gd")

    emit_w2h_build(p, nc, w2t, t1_dram, w2h_dram, h128b, h64b, ps_a, ps_b)

    def per_tb(tb, x_fm, macc, bufs):
        emit_swiglu_block(p, nc, bufs, x_fm, macc, w1t, w3t, w2h_dram.ap(),
                          8192)

    emit_swiglu_driver(p, nc, xq, xout, ident, mst,
                       [ps_a, ps_b, ps_c, ps_tr], per_tb)
    if dev:
        w2hd = nc.dram_tensor("w2hd", [8192, D], BF16, kind="ExternalOutput").ap()
        p.barrier(("gpsimd",))
        p.step("gpsimd", lambda E: E.dma_start(out=w2hd[:, :], in_=w2h_dram.ap()),
               inc="gd", inc_by=16)
        p.wait("gpsimd", "gd")
    es.close()


def emit_w2h_build(p, nc, w2t, t1_dram, w2h_dram, h128b, h64b, ps_a, ps_b,
                   serial=True):
    FWHT_SC = float(1.0 / np.sqrt(8192.0))
    with (
        nc.sbuf_tensor(_uq("wh_ld"), [128, D], BF16) as wld,
        nc.sbuf_tensor(_uq("wh_ev"), [128, D], BF16) as wev,
        nc.psum_tensor(_uq("wh_ps"), [128, D], mybir.dt.float32) as ps_w,
    ):
        def stage(ldq, load_fn, mm_fn, ev_fn, store_fn):
            lsem = "ld" if ldq == "sync" else "gd"
            p.step(ldq, load_fn, inc=lsem, inc_by=16)
            p.wait("tensor", lsem)
            for f in mm_fn:
                p.step("tensor", f, inc="pe")
            p.wait("scalar", "pe")
            p.step("scalar", ev_fn, inc="se")
            p.wait("sync", "se")
            p.step("sync", store_fn, inc="st", inc_by=16)
            p.barrier(("sync", "gpsimd", "tensor", "scalar"))

        for i in range(64):
            mms = [
                (lambda E, db=db: E.matmul(
                    ps_w[:, db * 512:(db + 1) * 512],
                    h128b[:, :], wld[:, db * 512:(db + 1) * 512],
                    start=True, stop=True))
                for db in range(4)]
            stage("gpsimd",
                  lambda E, i=i: E.dma_start(
                      out=wld[:, :],
                      in_=bass.AP(w2t.tensor, i * 128 * D, [[D, 128], [1, D]])),
                  mms,
                  lambda E: E.activation(wev[:, :], ps_w[:, :],
                                         AF.Identity, scale=FWHT_SC),
                  lambda E, i=i: E.dma_start(
                      out=bass.AP(t1_dram, i * D, [[64 * D, 128], [1, D]]),
                      in_=wev[:, :]))
        for jp in range(128):
            mms = [
                (lambda E, db=db: E.matmul(
                    ps_w[0:64, db * 512:(db + 1) * 512],
                    h64b[:, :], wld[0:64, db * 512:(db + 1) * 512],
                    start=True, stop=True))
                for db in range(4)]
            stage("sync",
                  lambda E, jp=jp: E.dma_start(
                      out=wld[0:64, :],
                      in_=bass.AP(t1_dram, jp * 64 * D, [[D, 64], [1, D]])),
                  mms,
                  lambda E: E.activation(wev[0:64, :], ps_w[0:64, :],
                                         AF.Identity),
                  lambda E, jp=jp: E.dma_start(
                      out=bass.AP(w2h_dram, jp * D, [[128 * D, 64], [1, D]]),
                      in_=wev[0:64, :]))
        p.barrier()



def declare_moe_inputs(nc, shared):
    a = {}
    a["xq"] = shared.get("xq") or nc.dram_tensor(
        "xq", [TOK, D], F32, kind="ExternalInput").ap()
    a["ident_in"] = shared.get("ident") or nc.dram_tensor(
        "ident", [128, 128], F32, kind="ExternalInput").ap()
    a["sw1t"] = nc.dram_tensor("sw1t", [D, 8192], F32, kind="ExternalInput").ap()
    a["sw2t"] = nc.dram_tensor("sw2t", [D, 8192], F32, kind="ExternalInput").ap()
    a["sw3t"] = nc.dram_tensor("sw3t", [8192, D], F32, kind="ExternalInput").ap()
    a["w12t"] = nc.dram_tensor("w12t", [8, D, 2048], F32,
                               kind="ExternalInput").ap()
    a["w3et"] = nc.dram_tensor("w3et", [8, 1024, D], F32,
                               kind="ExternalInput").ap()
    a["rw_lay"] = nc.dram_tensor("rw_lay", [128, 128], F32,
                                 kind="ExternalInput").ap()
    a["ebias"] = nc.dram_tensor("ebias", [1, 8], F32, kind="ExternalInput").ap()
    return a


def build_moe_nc(dev=False):
    nc = bass.Bass("TRN2", target_bir_lowering=False, debug=False)
    p = Prog(nc)
    aps = declare_moe_inputs(nc, {})
    xout = nc.dram_tensor("xout", [TOK, D], F32, kind="ExternalOutput").ap()
    emit_moe(p, nc, aps, xout, dev=dev)
    p.wait("gpsimd", "st")
    return p.finalize()


def emit_moe(p, nc, aps, xout, dev=False):
    import contextlib
    es = contextlib.ExitStack()
    xq, ident_in = aps["xq"], aps["ident_in"]
    sw1t, sw2t, sw3t = aps["sw1t"], aps["sw2t"], aps["sw3t"]
    w12t, w3et = aps["w12t"], aps["w3et"]
    rw_in, eb_in = aps["rw_lay"], aps["ebias"]
    if dev:
        combdump = nc.dram_tensor("combdump", [4, 128, 32], F32,
                                  kind="ExternalOutput").ap()
        maccdump = nc.dram_tensor("maccdump", [4, 128, 4 * D], F32,
                                  kind="ExternalOutput").ap()

    def SB(name, shape, dt):
        return es.enter_context(nc.sbuf_tensor(_uq(name), shape, dt))

    def PS(name, shape, dt=F32):
        return es.enter_context(nc.psum_tensor(_uq(name), shape, dt))

    ident = SB("identb", [128, 128], BF16)
    ident32 = SB("ident32", [128, 128], F32)
    mst = SB("mst", [128, 16], F32)
    rw_sb = SB("rw_sb", [128, 128], F32)
    xfm32 = SB("xfm32", [128, 16 * 128], F32)
    ebb = SB("ebb", [128, 8], F32)
    ebr = SB("ebr", [1, 8], F32)
    ones = SB("ones", [1, 128], F32)
    scor = SB("scor", [128, 8 * 8], F32)
    comb = SB("comb", [128, 4 * 8], F32)

    ps_a = PS("ps_a", [128, 512])
    ps_b = PS("ps_b", [128, 512])
    ps_c = PS("ps_c", [128, 512])
    ps_tr = PS("ps_tr", [128, 512], BF16)

    p.step("gpsimd", lambda E: E.dma_start(out=ident[:, :], in_=ident_in[:, :]),
           inc="gd", inc_by=16)
    p.step("sync", lambda E: E.dma_start(out=rw_sb[:, :], in_=rw_in[:, :]),
           inc="ld", inc_by=16)
    p.step("sync", lambda E: E.dma_start(out=ident32[:, :], in_=ident_in[:, :]),
           inc="ld", inc_by=16)
    p.step("sync", lambda E: E.dma_start(out=ebr[:, :], in_=eb_in[:, :]),
           inc="ld", inc_by=16)
    p.step("vector", lambda E: E.memset(ones[:, :], 1.0), inc="ve")
    p.wait("tensor", "gd")
    p.wait("tensor", "ld")
    p.wait("tensor", "ve")
    p.step("tensor", lambda E: E.matmul(
        ps_a[:, 0:8], ones[0:1, 0:128], ebr[0:1, 0:8],
        start=True, stop=True), inc="pe")
    p.wait("vector", "pe")
    p.step("vector", lambda E: E.tensor_copy(ebb[:, :], ps_a[:, 0:8]), inc="ve")
    p.wait("tensor", "ve")

    def vstep(f):
        p.step("vector", f, inc="ve")
        p.wait("vector", "ve")

    def per_tb(tb, x_fm, macc, bufs):
        for tsub in range(4):
            for c0 in range(0, 16, 4):
                p.wait("tensor", "ve")
                for u in range(4):
                    p.step("tensor", lambda E, c=c0 + u, u=u, tsub=tsub:
                           E.transpose(
                               ps_a[:, u * 128:(u + 1) * 128],
                               bufs["x_tm"][:, tsub * D + c * 128:
                                            tsub * D + (c + 1) * 128],
                               ident32[:, :]),
                           inc="pe" if u == 3 else None)
                p.wait("vector", "pe")
                p.step("vector", lambda E, c0=c0: E.tensor_copy(
                    xfm32[:, c0 * 128:(c0 + 4) * 128], ps_a[:, 0:512]),
                    inc="ve")
            p.wait("tensor", "ve")
            p.wait("tensor", "se")
            for c in range(16):
                p.step("tensor", lambda E, c=c: E.matmul(
                    ps_b[:, 0:8],
                    xfm32[:, c * 128:(c + 1) * 128],
                    rw_sb[:, c * 8:(c + 1) * 8],
                    start=(c == 0), stop=(c == 15)),
                    inc="pe" if c == 15 else None)
            p.wait("scalar", "pe")
            p.wait("scalar", "ve")
            p.step("scalar", lambda E: E.activation(
                scor[:, 0:8], ps_b[:, 0:8], AF.Sigmoid), inc="se")
            p.wait("vector", "se")
            vstep(lambda E: E.tensor_add(scor[:, 8:16], scor[:, 0:8], ebb[:, :]))
            vstep(lambda E: E.reduce_max(mst[:, 8:9], scor[:, 8:16], axis=AX.X))
            vstep(lambda E: E.tensor_scalar(
                scor[:, 16:24], scor[:, 8:16], mst[:, 8:9], None, ALU.subtract))
            vstep(lambda E: E.tensor_single_scalar(
                scor[:, 16:24], scor[:, 16:24], 0.0, ALU.is_ge))
            vstep(lambda E: E.scalar_tensor_tensor(
                scor[:, 24:32], scor[:, 16:24], NEG, scor[:, 8:16],
                ALU.mult, ALU.add))
            vstep(lambda E: E.reduce_max(mst[:, 9:10], scor[:, 24:32], axis=AX.X))
            vstep(lambda E: E.tensor_scalar(
                scor[:, 32:40], scor[:, 8:16], mst[:, 9:10], None, ALU.subtract))
            vstep(lambda E: E.tensor_single_scalar(
                scor[:, 32:40], scor[:, 32:40], 0.0, ALU.is_ge))
            vstep(lambda E: E.tensor_mul(scor[:, 40:48], scor[:, 0:8],
                                         scor[:, 32:40]))
            vstep(lambda E: E.reduce_sum(mst[:, 10:11], scor[:, 40:48],
                                         axis=AX.X))
            vstep(lambda E: E.tensor_scalar_add(mst[:, 10:11], mst[:, 10:11],
                                                1e-9))
            vstep(lambda E: E.reciprocal(mst[:, 11:12], mst[:, 10:11]))
            vstep(lambda E, tsub=tsub: E.tensor_scalar_mul(
                comb[:, tsub * 8:(tsub + 1) * 8], scor[:, 40:48],
                mst[:, 11:12]))
        p.barrier()
        if dev:
            p.wait("sync", "ve")
            p.step("sync", lambda E, tb=tb: E.dma_start(
                out=combdump[tb, :, :], in_=comb[:, :]),
                inc="st", inc_by=16)
            p.wait("vector", "st")

        p.barrier()
        emit_swiglu_block(p, nc, bufs, x_fm, macc, sw1t, sw2t, sw3t, 8192)
        for e in range(8):
            p.barrier()
            emit_swiglu_block(
                p, nc, bufs, x_fm, macc, w12t, w12t, w3et, 1024,
                up_off1=0, up_off3=1024, up_base=e * D * 2048,
                down_base=e * 1024 * D, up_stride=2048, down_stride=D,
                acc_scale=lambda tsub, e=e: comb[:, tsub * 8 + e:tsub * 8 + e + 1])
        p.barrier()

    emit_swiglu_driver(p, nc, xq, xout, ident, mst,
                       [ps_a, ps_b, ps_c, ps_tr], per_tb,
                       macc_dump=(maccdump if dev else None))
    es.close()


def moe_inputs(params, x_slice):
    rw = params["L1_ffn_router_w"]
    rw_lay = np.zeros((128, 128), np.float32)
    for c in range(16):
        rw_lay[:, c * 8:(c + 1) * 8] = rw[:, c * 128:(c + 1) * 128].T
    return {
        "xq": x_slice,
        "ident": np.eye(128, dtype=np.float32),
        "sw1t": np.ascontiguousarray(params["L1_ffn_sw1"].T),
        "sw2t": np.ascontiguousarray(params["L1_ffn_sw2"].T),
        "sw3t": np.ascontiguousarray(params["L1_ffn_sw3"].T),
        "w12t": np.ascontiguousarray(params["L1_ffn_w12"].transpose(0, 2, 1)),
        "w3et": np.ascontiguousarray(params["L1_ffn_w3"].transpose(0, 2, 1)),
        "rw_lay": rw_lay,
        "ebias": params["L1_ffn_expert_bias"].reshape(1, 8),
    }



def _own_rows(xb_full, g):
    return np.ascontiguousarray(
        xb_full.reshape(S // 128, 128, -1)[g::4].reshape(TOK, -1))


def _scatter_rows(dst, rows, g):
    dst.reshape(S // 128, 128, -1)[g::4] = rows.reshape(NT, 128, -1)


def attention_inputs(layer, x_full, params, g):
    ident = np.eye(128, dtype=np.float32)
    masks = _diag_masks(g)
    d = {
        "xb": x_full,
        "xq": _own_rows(x_full, g),
        "ident": ident,
        "masks": masks,
    }
    if layer == 0:
        wqkv = params["L0_attn_Wqkv_w"]
        d["wqkv_t"] = np.ascontiguousarray(wqkv.T)
        d["wout_t"] = np.ascontiguousarray(params["L0_attn_out_w"].T)
        bq = params["L0_attn_Wqkv_b"]
        bias = np.zeros((128, 32), np.float32)
        bias[:, 0:16] = bq[0:2048].reshape(16, 128).T
        bias[:, 16:24] = bq[2048:3072].reshape(8, 128).T
        d["bqkv"] = bias
        bvr = np.zeros((1, D), np.float32)
        bvr[0, 0:1024] = bq[3072:4096]
        d["bv_row"] = bvr
        d["bout"] = params["L0_attn_out_b"].reshape(1, D)
        d["lamv"] = np.concatenate(
            [params["L0_attn_lambda_q1"], params["L0_attn_lambda_k1"],
             params["L0_attn_lambda_q2"], params["L0_attn_lambda_k2"]]
        ).astype(np.float32).reshape(1, 512)
        d["subw"] = params["L0_attn_subln_w"].reshape(1, 256)
    else:
        d["wq"] = params["L1_attn_Wq"]
        d["wk"] = params["L1_attn_Wk"]
        d["wv"] = params["L1_attn_Wv"]
        d["wo"] = params["L1_attn_Wo"]
        bias = np.zeros((128, 32), np.float32)
        bias[:, 0:16] = params["L1_attn_bq"].reshape(16, 128).T
        bias[:, 16:32] = params["L1_attn_bk"].reshape(16, 128).T
        d["bqkv"] = bias
        d["bv_row"] = params["L1_attn_bv"].reshape(1, D)
        d["bout"] = params["L1_attn_bo"].reshape(1, D)
        d["alibi_f"] = _alibi_rel()
    return d


def build_layerA_nc():
    nc = bass.Bass("TRN2", target_bir_lowering=False, debug=False)
    p = Prog(nc)
    a_aps = declare_attention_inputs(nc, 0, {})
    f_aps = declare_ffn_inputs(nc, {"xq": "placeholder", "ident": a_aps["ident_in"]})
    x1_scr = nc.dram_tensor("x1_scr", [TOK, D], F32, kind="Internal")
    f_aps["xq"] = x1_scr.ap()
    f_aps["ident_in"] = a_aps["ident_in"]
    xout = nc.dram_tensor("xout", [TOK, D], F32, kind="ExternalOutput").ap()
    emit_attention(p, nc, 0, a_aps, x1_scr.ap())
    p.barrier()
    emit_ffn(p, nc, f_aps, xout)
    p.wait("gpsimd", "st")
    return p.finalize()


def build_layerB_nc():
    nc = bass.Bass("TRN2", target_bir_lowering=False, debug=False)
    p = Prog(nc)
    a_aps = declare_attention_inputs(nc, 1, {})
    m_aps = declare_moe_inputs(nc, {"xq": "placeholder",
                                    "ident": a_aps["ident_in"]})
    x3_scr = nc.dram_tensor("x3_scr", [TOK, D], F32, kind="Internal")
    m_aps["xq"] = x3_scr.ap()
    m_aps["ident_in"] = a_aps["ident_in"]
    xout = nc.dram_tensor("xout", [TOK, D], F32, kind="ExternalOutput").ap()
    emit_attention(p, nc, 1, a_aps, x3_scr.ap())
    p.barrier()
    emit_moe(p, nc, m_aps, xout)
    p.wait("gpsimd", "st")
    return p.finalize()


_FFN_INPUTS_CACHE = {}


def ffn_inputs(params, x_slice):
    return {
        "xq": x_slice,
        "ident": np.eye(128, dtype=np.float32),
        "w1t": np.ascontiguousarray(params["L0_ffn_W1"].T),
        "w3t": np.ascontiguousarray(params["L0_ffn_W3"].T),
        "w2t": np.ascontiguousarray(params["L0_ffn_W2"].T),
        "h128": _sylvester(128), "h64": _sylvester(64),
    }


def _gather_full(shards):
    out = np.empty((B, S, D), np.float32)
    for c in range(8):
        b, g = c // 4, c % 4
        out[b].reshape(S // 128, 128, D)[g::4] = shards[c].reshape(NT, 128, D)
    return out


def kernel(**inputs):
    x0 = np.ascontiguousarray(inputs["L0_attn_hidden_states"], dtype=np.float32)
    params = inputs

    fw = {
        "w1t": np.ascontiguousarray(params["L0_ffn_W1"].T),
        "w3t": np.ascontiguousarray(params["L0_ffn_W3"].T),
        "w2t": np.ascontiguousarray(params["L0_ffn_W2"].T),
        "h128": _sylvester(128), "h64": _sylvester(64),
    }
    nc = build_layerA_nc()
    im = []
    for c in range(8):
        b, g = c // 4, c % 4
        d = attention_inputs(0, np.ascontiguousarray(x0[b]), params, g)
        d.update(fw)
        im.append(d)
    res = run_bass_kernel_spmd(nc, im, core_ids=list(range(8)))
    x2_own = [np.asarray(res.results[c]["xout"]) for c in range(8)]
    x2_full = _gather_full(x2_own)

    nc = build_layerB_nc()
    im = []
    for c in range(8):
        b, g = c // 4, c % 4
        d = attention_inputs(1, np.ascontiguousarray(x2_full[b]), params, g)
        d["xq"] = x2_own[c]
        d.update(moe_inputs(params, x2_own[c]))
        d["xq"] = x2_own[c]
        im.append(d)
    res = run_bass_kernel_spmd(nc, im, core_ids=list(range(8)))
    out_own = [np.asarray(res.results[c]["xout"]) for c in range(8)]
    return _gather_full(out_own)


def kernel_4launch(**inputs):
    x0 = np.ascontiguousarray(inputs["L0_attn_hidden_states"], dtype=np.float32)
    params = inputs

    nc = build_attention_nc(0)
    im = []
    for c in range(8):
        b, g = c // 4, c % 4
        im.append(attention_inputs(0, np.ascontiguousarray(x0[b]), params, g))
    res = run_bass_kernel_spmd(nc, im, core_ids=list(range(8)))
    x1_own = [np.asarray(res.results[c]["xout"]) for c in range(8)]

    nc = build_ffn_nc()
    im = []
    fw = {
        "ident": np.eye(128, dtype=np.float32),
        "w1t": np.ascontiguousarray(params["L0_ffn_W1"].T),
        "w3t": np.ascontiguousarray(params["L0_ffn_W3"].T),
        "w2t": np.ascontiguousarray(params["L0_ffn_W2"].T),
        "h128": _sylvester(128), "h64": _sylvester(64),
    }
    for c in range(8):
        im.append(dict(fw, xq=x1_own[c]))
    res = run_bass_kernel_spmd(nc, im, core_ids=list(range(8)))
    x2_own = [np.asarray(res.results[c]["xout"]) for c in range(8)]
    x2_full = _gather_full(x2_own)

    nc = build_attention_nc(1)
    im = []
    for c in range(8):
        b, g = c // 4, c % 4
        d = attention_inputs(1, np.ascontiguousarray(x2_full[b]), params, g)
        d["xq"] = x2_own[c]
        im.append(d)
    res = run_bass_kernel_spmd(nc, im, core_ids=list(range(8)))
    x3_own = [np.asarray(res.results[c]["xout"]) for c in range(8)]

    nc = build_moe_nc()
    im = [moe_inputs(params, x3_own[c]) for c in range(8)]
    res = run_bass_kernel_spmd(nc, im, core_ids=list(range(8)))
    out_own = [np.asarray(res.results[c]["xout"]) for c in range(8)]
    return _gather_full(out_own)
